# version 01
# baseline (speedup 1.0000x reference)
import os
import numpy as np
import concourse.bass as bass
import concourse.mybir as mybir
from concourse.bass_utils import run_bass_kernel_spmd

F32 = mybir.dt.float32
BF16 = mybir.dt.bfloat16
AF = mybir.ActivationFunctionType
ALU = mybir.AluOpType
AX = mybir.AxisListType

NTOK, D, NT, CBN, NH, DFF = 2048, 1024, 16, 8, 16, 4096
LN_EPS, B_EPS = 1e-5, 1e-12
SB_LIMIT = 212500



def _colmat(v):
    return np.ascontiguousarray(np.asarray(v, np.float32).reshape(-1, 128).T)


def host_prepare(inp):
    f32 = lambda k: np.asarray(inp[k], np.float32)
    c = {}
    c["wqkv"] = np.ascontiguousarray(f32("L0_attn_qkv_w"))
    c["woutA"] = np.ascontiguousarray(f32("L0_attn_out_w"))
    c["w1A"] = np.ascontiguousarray(f32("L0_attn_w1"))
    c["w2A"] = np.ascontiguousarray(f32("L0_attn_w2"))
    c["W1T"] = np.ascontiguousarray(np.transpose(f32("L0_ffn_W1"), (0, 2, 1)))
    c["W2T"] = np.ascontiguousarray(np.transpose(f32("L0_ffn_W2"), (0, 2, 1)))
    c["WrT"] = np.ascontiguousarray(f32("L0_ffn_Wr").T)
    c["qwT"] = np.ascontiguousarray(f32("L1_attn_q_w").T)
    c["kwT"] = np.ascontiguousarray(f32("L1_attn_k_w").T)
    c["vwT"] = np.ascontiguousarray(f32("L1_attn_v_w").T)
    c["owT"] = np.ascontiguousarray(f32("L1_attn_out_w").T)
    c["bwT"] = np.ascontiguousarray(f32("L1_ffn_base_w").T)
    c["Acat"] = np.ascontiguousarray(np.transpose(f32("L1_ffn_lora_A"), (1, 0, 2)).reshape(D, 128))
    c["Bcat"] = np.ascontiguousarray(f32("L1_ffn_lora_B").reshape(128, D))
    c["rwT"] = np.ascontiguousarray(f32("L1_ffn_router_w").T)
    c["ident"] = np.eye(128, dtype=np.float32)

    cols, colmap = [], {}

    def addcol(name, v):
        m = _colmat(v)
        colmap[name] = (sum(x.shape[1] for x in cols), m.shape[1])
        cols.append(m)

    addcol("g1", inp["L0_attn_g1"]); addcol("be1", inp["L0_attn_be1"])
    addcol("g2", inp["L0_attn_g2"]); addcol("be2", inp["L0_attn_be2"])
    addcol("qbA", f32("L0_attn_qkv_b")[0:1024])
    addcol("b1A", inp["L0_attn_b1"])
    for e in range(8):
        addcol(f"b1B{e}", f32("L0_ffn_b1")[e])
    addcol("qbC", inp["L1_attn_q_b"]); addcol("kbC", inp["L1_attn_k_b"])
    cw3 = f32("L1_attn_conv_w").reshape(D, 3)
    addcol("cw0", cw3[:, 0]); addcol("cw1", cw3[:, 1]); addcol("cw2", cw3[:, 2])
    addcol("cbb", inp["L1_attn_conv_b"])
    c["colatlas"] = np.concatenate(cols, axis=1).astype(np.float32)

    rows, rowmap = [], {}

    def addrow(name, v):
        v = np.asarray(v, np.float32).reshape(1, -1)
        rowmap[name] = (sum(x.shape[1] for x in rows), v.shape[1])
        rows.append(v)

    addrow("kvbA", f32("L0_attn_qkv_b")[1024:3072])
    addrow("b2A", inp["L0_attn_b2"]); addrow("outbA", inp["L0_attn_out_b"])
    addrow("brB", inp["L0_ffn_br"])
    for e in range(8):
        addrow(f"b2B{e}", f32("L0_ffn_b2")[e])
    addrow("lnwB", inp["L0_ffn_ln_w"]); addrow("lnbB", inp["L0_ffn_ln_b"])
    addrow("vbC", inp["L1_attn_v_b"]); addrow("outbC", inp["L1_attn_out_b"])
    addrow("bbD", inp["L1_ffn_base_b"]); addrow("brD", inp["L1_ffn_router_b"])
    c["rowatlas"] = np.concatenate(rows, axis=1).astype(np.float32)

    xs = [np.ascontiguousarray(f32("L0_attn_src")[b]) for b in range(8)]
    return c, xs, colmap, rowmap



class Sem:
    def __init__(self, h):
        self.h, self.val = h, 0

    def inc(self, inst, n=1):
        self.val += n
        inst.then_inc(self.h, n)
        return self.val


class MK:
    def __init__(self, nc, colmap, rowmap):
        self.nc, self.colmap, self.rowmap = nc, colmap, rowmap
        self.d, self.s, self.sem = {}, {}, {}
        self.pb, self.block = [], None
        self.tb_free = 0
        self.stage_gate = None
        self.hdma_last = {}

    def col(self, name, blk=0, n=1):
        off, cnt = self.colmap[name]
        assert blk + n <= cnt, name
        return self.s["colatlas"][:, off + blk: off + blk + n]

    def rowd(self, name, lo=0, n=None):
        off, cnt = self.rowmap[name]
        if n is None:
            n = cnt - lo
        return self.d["rowatlas"][:, off + lo: off + lo + n]

    def sb(self, name, shape, dtype, off):
        bpe = {F32: 4, BF16: 2}[dtype]
        n = 1
        for x in shape[1:]:
            n *= x
        assert off + n * bpe <= SB_LIMIT, f"{name}: {off + n * bpe}"
        return self.nc.alloc_sbuf_tensor_at(name, list(shape), dtype,
                                            offset=self.base + off).ap()


NTOT = 212352


def build_program(colmap, rowmap, stages=("A", "B", "C", "D")):
    nc = bass.Bass("TRN2", target_bir_lowering=False, debug=False,
                   detect_race_conditions=False)
    mk = MK(nc, colmap, rowmap)
    mk.acut = os.environ.get("MK_ACUT", "")
    arena = nc.alloc_sbuf_tensor("arena0", [128, NTOT], mybir.dt.uint8)
    mk.base = nc.lookup_mloc(arena).addr
    assert mk.base + NTOT <= 229376, mk.base
    d, s = mk.d, mk.s

    d["x"] = nc.dram_tensor("x", [NTOK, D], F32, kind="ExternalInput").ap()
    d["out"] = nc.dram_tensor("out", [NTOK, D], F32, kind="ExternalOutput").ap()
    shapes = dict(wqkv=[D, 3 * D], woutA=[D, D], w1A=[D, DFF], w2A=[DFF, D],
                  WrT=[D, 8], W1T=[8, D, DFF], W2T=[8, DFF, D],
                  qwT=[D, D], kwT=[D, D], vwT=[D, D], owT=[D, D],
                  bwT=[D, D], Acat=[D, 128], Bcat=[128, D], rwT=[D, 8],
                  ident=[128, 128])
    for nm, shp in shapes.items():
        d[nm] = nc.dram_tensor(nm, shp, F32, kind="ExternalInput").ap()
    ncols = sum(v[1] for v in colmap.values())
    nrows = sum(v[1] for v in rowmap.values())
    d["colatlas"] = nc.dram_tensor("colatlas", [128, ncols], F32, kind="ExternalInput").ap()
    d["rowatlas"] = nc.dram_tensor("rowatlas", [1, nrows], F32, kind="ExternalInput").ap()
    d["xspill"] = nc.dram_tensor("xspill", [NTOK, D], F32, kind="Internal").ap()
    d["x1spill"] = nc.dram_tensor("x1spill", [NTOK, D], F32, kind="Internal").ap()
    d["tb"] = nc.dram_tensor("tb", [NTOK, D], BF16, kind="Internal").ap()
    d["hsp"] = nc.dram_tensor("hsp", [32, 128, NTOK], BF16, kind="Internal").ap()

    o = 0

    def perm(name, shape, dtype):
        nonlocal o
        t = mk.sb(name, shape, dtype, o)
        bpe = {F32: 4, BF16: 2}[dtype]
        n = 1
        for x in shape[1:]:
            n *= x
        o += ((n * bpe + 31) // 32) * 32
        return t

    s["colatlas"] = perm("colatlas", [128, ncols], F32)
    s["rowbuf"] = perm("rowbuf", [1, 2112], BF16)
    s["ones_row"] = perm("ones_row", [1, 512], BF16)
    s["ones_c1"] = perm("ones_c1", [1, 128], BF16)
    s["ones_c1f"] = perm("ones_c1f", [1, 128], F32)
    s["onep64"] = perm("onep64", [128, 64], F32)
    s["stats"] = perm("stats", [128, 64], F32)
    s["stats2"] = perm("stats2", [128, 48], F32)
    s["cw"] = perm("cw", [128, 128], F32)
    s["cwD"] = perm("cwD", [128, 128], F32)
    s["kvsb"] = perm("kvsb", [128, 544], BF16)
    s["wr8"] = perm("wr8", [128, 64], BF16)
    s["ident"] = perm("ident", [128, 128], BF16)
    s["ishift"] = perm("ishift", [128, 128], BF16)
    s["g8"] = perm("g8", [128, 48], F32)
    s["const32"] = perm("const32", [128, 8], F32)
    mk.constvals = [1e-5, 1e-12, -0.5, 0.125]
    for ci, cv in enumerate(mk.constvals):
        nc.const_aps.register(F32, cv, s["const32"][:, ci:ci + 1])
    P = ((o + 127) // 128) * 128
    mk.P = P
    assert P <= 11500, P

    mk.pb = [nc.alloc_psum_tensor(f"pb{i}", [128, 512], F32).ap() for i in range(8)]
    s["xres"] = mk.sb("xres", [128, NT * D], F32, P)

    cms = []
    for nm in ["g", "sy", "mm", "sc", "v"]:
        cm = nc.semaphore(f"sem_{nm}")
        mk.sem[nm] = Sem(cm.__enter__())
        cms.append(cm)

    blk = nc.Block()
    mk.block = blk.__enter__()

    emit_preload(mk)
    if "A" in stages:
        emit_stage_A(mk)
    else:
        emit_passthrough_load(mk)
    if "B" in stages:
        emit_stage_B(mk)
    if "C" in stages:
        emit_stage_C(mk)
    if "D" in stages:
        emit_stage_D(mk)
    emit_finish(mk)

    blk.__exit__(None, None, None)
    for cm in reversed(cms):
        cm.__exit__(None, None, None)
    return nc, mk


def emit_preload(mk):
    block, s, d = mk.block, mk.s, mk.d
    SG, SS, SV = mk.sem["g"], mk.sem["sy"], mk.sem["v"]

    @block.sync
    def _(sy):
        SS.inc(sy.dma_start(out=s["colatlas"], in_=d["colatlas"]), 16)
    mk.atlas_in = SS.val

    @block.gpsimd
    def _(g):
        SG.inc(g.dma_start(out=s["ident"], in_=d["ident"]), 16)
    ident_in = SG.val

    @block.vector
    def _(v):
        v.wait_ge(SG.h, ident_in)
        v.memset(s["ones_row"], 1.0)
        v.memset(s["ones_c1"], 1.0)
        v.memset(s["ones_c1f"], 1.0)
        v.memset(s["onep64"][64:65, :], 1.0)
        v.memset(s["kvsb"], 0.0)
        for ci, cv in enumerate(mk.constvals):
            v.memset(s["const32"][:, ci:ci + 1], cv)
        SV.inc(v.memset(s["ishift"], 0.0), 1)
        v.wait_ge(SV.h, SV.val)
        inst = v.tensor_copy(s["ishift"][0:64, 64:128], s["ident"][0:64, 0:64])
        SV.inc(inst, 1)
    mk.preload_v = SV.val


def emit_passthrough_load(mk):
    block, d = mk.block, mk.d
    SS, SV = mk.sem["sy"], mk.sem["v"]
    xsb = mk.s["xres"]

    @block.sync
    def _(sy):
        SS.inc(sy.dma_start(out=xsb.rearrange("p (t c) -> p t c", c=D),
                            in_=d["x"].rearrange("(t p) c -> p t c", p=128)), 16)
    x_in = SS.val

    @block.vector
    def _(v):
        v.wait_ge(SS.h, x_in)
        SV.inc(v.engine_nop(), 1)
    mk.stage_gate = (SV, SV.val)



def emit_ln_stats(mk, x_tile, junk_ap, eps, wait_sems=(), extra=None):
    block = mk.block
    S, S2 = mk.s["stats"], mk.s["stats2"]
    SC, SV = mk.sem["sc"], mk.sem["v"]

    @block.vector
    def _(v):
        for se, val in wait_sems:
            v.wait_ge(se.h, val)
        for i in range(NT):
            inst = v.reduce_sum(S[:, i:i + 1], x_tile(i), axis=AX.X)
        SV.inc(inst, 1)
    v_sum = SV.val

    @block.scalar
    def _(sc):
        for se, val in wait_sems:
            sc.wait_ge(se.h, val)
        for i in range(NT):
            inst = sc.activation(junk_ap, x_tile(i), AF.Square, accum_out=S[:, 16 + i:17 + i])
        SC.inc(inst, 1)
    c_ssq = SC.val

    if mk.nc and getattr(mk, "acut", "") == "stats0":
        @block.vector
        def _(v):
            v.wait_ge(SC.h, c_ssq)
            v.wait_ge(SV.h, v_sum)
            inst = v.tensor_copy(mk.s["xres"][:, 0:64], S)
            SV.inc(inst, 1)
        mk.stage_gate = (SV, SV.val)
        mk._cut_hit = True
        return SV.val

    @block.vector
    def _(v):
        v.wait_ge(SC.h, c_ssq)
        v.wait_ge(SV.h, v_sum)
        SV.inc(v.tensor_scalar_mul(S[:, 32:48], S[:, 0:16], 1.0 / D), 1)
        v.wait_ge(SV.h, SV.val)
        SV.inc(v.tensor_mul(S[:, 48:64], S[:, 32:48], S[:, 32:48]), 1)
        v.wait_ge(SV.h, SV.val)
        inst = v.scalar_tensor_tensor(S2[:, 32:48], S[:, 16:32], 1.0 / D, S[:, 48:64],
                                      ALU.mult, ALU.subtract)
        SV.inc(inst, 1)

    @block.scalar
    def _(sc):
        sc.wait_ge(SV.h, SV.val)
        SC.inc(sc.activation(S[:, 48:64], S2[:, 32:48], AF.Ln, bias=float(eps)), 1)
        sc.wait_ge(SC.h, SC.val)
        inst = sc.activation(S2[:, 0:16], S[:, 48:64], AF.Exp, scale=-0.5)
        SC.inc(inst, 1)

    @block.vector
    def _(v):
        v.wait_ge(SC.h, SC.val)
        inst = v.scalar_tensor_tensor(S2[:, 16:32], S[:, 32:48], -1.0, S2[:, 0:16],
                                      ALU.mult, ALU.mult)
        if extra is not None:
            def sync():
                SV.inc(v.engine_nop(), 0) if False else None
            inst = extra(v, SV)
        SV.inc(inst, 1)
    return SV.val


def emit_ln_apply_bounce(mk, x_tile, stg32, dstT, stats_ev):
    block, d = mk.block, mk.d
    SG, SS, SC = mk.sem["g"], mk.sem["sy"], mk.sem["sc"]
    SV, S2 = mk.sem["v"], mk.s["stats2"]
    C0 = SC.val
    g0 = SG.val
    e_dma = [g0 + 16 * (i + 1) for i in range(NT)]
    tbf = mk.tb_free

    @block.scalar
    def _(sc):
        sc.wait_ge(SV.h, stats_ev)
        for i in range(NT):
            if i >= 4:
                sc.wait_ge(SG.h, e_dma[i - 4])
            inst = sc.activation(stg32[:, (i % 4) * 1024:(i % 4) * 1024 + 1024],
                                 x_tile(i), AF.Identity,
                                 bias=S2[:, 16 + i:17 + i], scale=S2[:, i:i + 1])
            SC.inc(inst, 1)

    @block.gpsimd
    def _(g):
        if tbf:
            g.wait_ge(SS.h, tbf)
        for i in range(NT):
            g.wait_ge(SC.h, C0 + i + 1)
            SG.inc(g.dma_start(out=d["tb"][i * 128:(i + 1) * 128, :],
                               in_=stg32[:, (i % 4) * 1024:(i % 4) * 1024 + 1024]), 16)
    sg_done = SG.val

    @block.sync
    def _(sy):
        sy.wait_ge(SG.h, sg_done)
        for cb in range(CBN):
            SS.inc(sy.dma_start(out=dstT(cb), in_=d["tb"][:, cb * 128:(cb + 1) * 128],
                                transpose=True), 16)
    mk.tb_free = SS.val
    return SS.val, sg_done


def bounce_transpose(mk, src3, dst_cb, wait=None, ncols=D):
    block, d = mk.block, mk.d
    SS = mk.sem["sy"]
    nblk = ncols // 128

    @block.sync
    def _(sy):
        if wait is not None:
            sy.wait_ge(wait[0].h, wait[1])
        if mk.tb_free:
            sy.wait_ge(SS.h, mk.tb_free)
        SS.inc(sy.dma_start(out=d["tb"].rearrange("(t p) c -> p t c", p=128)[:, :, 0:ncols],
                            in_=src3), 16)
        wr = SS.val
        sy.wait_ge(SS.h, wr)
        for cb in range(nblk):
            SS.inc(sy.dma_start(out=dst_cb(cb), in_=d["tb"][:, cb * 128:(cb + 1) * 128],
                                transpose=True), 16)
    mk.tb_free = SS.val
    return SS.val



def emit_expert(mk, w1_dram3, w2_dram3, b1col, b2row_sb, rhsT, w1dbl, hstage,
                w2t, hstream, evac_cb, w1_gate=(), w2_gate=(), rhs_gate=(),
                evac_wait=None):
    nc, block, d, s, pb = mk.nc, mk.block, mk.d, mk.s, mk.pb
    SG, SS, SM, SC, SV = (mk.sem[k] for k in ["g", "sy", "mm", "sc", "v"])
    FC, FB, NCH = 8, 4, 4

    g1_base = SG.val
    mm_base = SM.val
    e_mm = {}
    jj = 0
    for fc in range(FC):
        for ch in range(NCH):
            for fb in range(FB):
                e_mm[(fc, ch, fb)] = mm_base + jj + 1
                jj += 1
    ph1_last = mm_base + FC * FB * NCH
    sc_base = SC.val
    e_gelu = {}
    jj = 0
    for fc in range(FC):
        for ch in range(NCH):
            for fb in range(FB):
                e_gelu[(fc, ch, fb)] = sc_base + jj + 1
                jj += 1

    @block.gpsimd
    def _(g):
        for se, val in w1_gate:
            g.wait_ge(se.h, val)
        for fc in range(FC):
            if fc >= 2:
                g.wait_ge(SM.h, e_mm[(fc - 2, NCH - 1, FB - 1)])
            SG.inc(g.dma_start(out=w1dbl[:, (fc % 2) * 4096:(fc % 2) * 4096 + 4096]
                               .rearrange("p (cb f) -> p cb f", f=512),
                               in_=w1_dram3[:, :, fc * 512:(fc + 1) * 512]), 16)
    e_w1 = [g1_base + 16 * (fc + 1) for fc in range(FC)]

    ss_h_base = SS.val
    e_hdma = {}
    k = 0
    for fc in range(FC):
        for ch in range(NCH):
            e_hdma[(fc, ch)] = ss_h_base + 16 * (k + 1)
            k += 1
    prev_hdma = dict(mk.hdma_last)

    def hdma_before(fc, ch):
        g = fc * NCH + ch - 2
        if g < 0:
            return prev_hdma.get(ch % 2)
        return e_hdma[(g // NCH, g % NCH)]

    @block.tensor
    def _(t):
        for se, val in rhs_gate:
            t.wait_ge(se.h, val)
        jj = 0
        for fc in range(FC):
            t.wait_ge(SG.h, e_w1[fc])
            for ch in range(NCH):
                for fb in range(FB):
                    if jj >= 4:
                        t.wait_ge(SC.h, sc_base + jj - 3)
                    ps = pb[jj % 4]
                    for cb in range(CBN):
                        inst = t.matmul(ps, w1dbl[:, (fc % 2) * 4096 + cb * 512 + fb * 128:
                                        (fc % 2) * 4096 + cb * 512 + (fb + 1) * 128],
                                        rhsT[:, cb * NTOK + ch * 512: cb * NTOK + (ch + 1) * 512],
                                        start=(cb == 0), stop=(cb == CBN - 1))
                    SM.inc(inst, 1)
                    jj += 1

    @block.scalar
    def _(sc):
        jj = 0
        for fc in range(FC):
            for ch in range(NCH):
                slot = (fc * NCH + ch) % 2
                for fb in range(FB):
                    sc.wait_ge(SM.h, e_mm[(fc, ch, fb)])
                    if fb == 0:
                        prev = hdma_before(fc, ch)
                        if prev is not None:
                            sc.wait_ge(SS.h, prev)
                    inst = sc.activation(
                        hstage[:, slot * 2048 + fb * 512: slot * 2048 + (fb + 1) * 512],
                        pb[jj % 4], AF.Gelu, bias=mk.col(b1col, fc * FB + fb))
                    SC.inc(inst, 1)
                    jj += 1

    @block.sync
    def _(sy):
        for fc in range(FC):
            for ch in range(NCH):
                slot = (fc * NCH + ch) % 2
                sy.wait_ge(SC.h, e_gelu[(fc, ch, FB - 1)])
                SS.inc(sy.dma_start(
                    out=d["hsp"][fc * 4:(fc + 1) * 4, :, ch * 512:(ch + 1) * 512]
                    .rearrange("fb p c -> p fb c"),
                    in_=hstage[:, slot * 2048:(slot + 1) * 2048]
                    .rearrange("p (fb c) -> p fb c", c=512)), 16)
    h_written = SS.val
    mk.hdma_last[0] = e_hdma[(FC - 1, NCH - 2)]
    mk.hdma_last[1] = e_hdma[(FC - 1, NCH - 1)]

    mm2_base = SM.val
    e_mm2 = [mm2_base + 4 * (w + 1) for w in range(8)]
    v2_base = SV.val
    e_ev2 = [v2_base + 4 * (w + 1) for w in range(8)]
    mk.e_ev2 = e_ev2
    ss2_base = SS.val
    e_hstr = [ss2_base + 16 * (w + 1) for w in range(8)]

    @block.sync
    def _(sy):
        sy.wait_ge(SS.h, h_written)
        for w in range(8):
            if w >= 2:
                sy.wait_ge(SM.h, e_mm2[w - 2])
            SS.inc(sy.dma_start(out=hstream[:, (w % 2) * 8192:(w % 2) * 8192 + 8192]
                                .rearrange("p (fb c) -> p fb c", c=256),
                                in_=d["hsp"][:, :, w * 256:(w + 1) * 256]
                                .rearrange("fb p c -> p fb c")), 16)

    @block.tensor
    def _(t):
        for se, val in w2_gate:
            t.wait_ge(se.h, val)
        t.wait_ge(SV.h, v2_base)
        for w in range(8):
            t.wait_ge(SS.h, e_hstr[w])
            if w >= 1:
                t.wait_ge(SV.h, e_ev2[w - 1])
            for ntl in range(2):
                for chk in range(2):
                    ps = pb[4 + ntl * 2 + chk]
                    for fb in range(32):
                        t.matmul(ps, hstream[:, (w % 2) * 8192 + fb * 256 + ntl * 128:
                                             (w % 2) * 8192 + fb * 256 + ntl * 128 + 128],
                                 w2t[:, fb * 1024 + chk * 512: fb * 1024 + (chk + 1) * 512],
                                 start=(fb == 0), stop=False)
                    inst = t.matmul(ps, s["ones_c1"], b2row_sb[:, chk * 512:(chk + 1) * 512],
                                    start=False, stop=True)
                    SM.inc(inst, 1)

    @block.vector
    def _(v):
        for w in range(8):
            if evac_wait is not None:
                for se, val in evac_wait(w):
                    v.wait_ge(se.h, val)
            for ntl in range(2):
                for chk in range(2):
                    v.wait_ge(SM.h, mm2_base + w * 4 + ntl * 2 + chk + 1)
                    inst = evac_cb(v, w, ntl, pb[4 + ntl * 2 + chk], chk)
                    SV.inc(inst, 1)
    assert SM.val == mm2_base + 32
    assert mm2_base == ph1_last
    return SV.val, ph1_last, e_mm2[-1]


def emit_dump_bf16(mk, src_full, ncols, gate):
    block = mk.block
    SV = mk.sem["v"]

    @block.vector
    def _(v):
        v.wait_ge(gate[0].h, gate[1])
        for j in range(0, ncols, 1024):
            w = min(1024, ncols - j)
            inst = v.tensor_copy(mk.s["xres"][:, j:j + w], src_full[:, j:j + w])
        SV.inc(inst, 1)
    mk.stage_gate = (SV, SV.val)



def emit_stage_A(mk):
    nc, block, d, s, pb = mk.nc, mk.block, mk.d, mk.s, mk.pb
    SG, SS, SM, SC, SV = (mk.sem[k] for k in ["g", "sy", "mm", "sc", "v"])
    P = mk.P
    S, S2 = s["stats"], s["stats2"]

    xsb = s["xres"]
    ln1 = mk.sb("ln1_A", [128, NT * D], BF16, P + 65536)
    R2 = mk.sb("R2_A", [128, NT * D], BF16, P + 98304)
    wsl = mk.sb("wsl_A", [128, NT * D], BF16, P + 131072)
    qfT = mk.sb("qfT_A", [128, NT * D], BF16, P + 163840)
    junk1 = mk.sb("junk1_A", [128, 2 * 1024], F32, P + 98304)
    stg32A = mk.sb("stg32_A", [128, 4 * 1024], F32, P + 106496)
    scr = mk.sb("scr_A", [128, 2048], F32, P)
    scr2 = mk.sb("scr2_A", [128, 2048], F32, P + 8192)
    scre = mk.sb("scre_A", [128, 2048], F32, P + 16384)
    kfv = mk.sb("kfv_A", [128, 2 * 1024 + 2 * 1040], BF16, P + 24576)
    junk2 = mk.sb("junk2_A", [128, 1024], F32, P + 98304)
    kvacc = mk.sb("kvacc_A", [128, 544], F32, P + 33024)
    s["kvacc"] = kvacc
    w1dbl = mk.sb("w1dbl_A", [128, 2 * 4096], BF16, P)
    hstage = mk.sb("hstage_A", [128, 2 * 2048], BF16, P + 16384)
    w2t = mk.sb("w2t_A", [128, 32 * 1024], BF16, P + 98304)
    hstream = mk.sb("hstream_A", [128, 2 * 8192], BF16, P + 65536)
    xx1 = [mk.sb(f"xx1_{k}_A", [128, 2 * 1024], F32, P + 163840 + k * 8192) for k in range(2)]

    def xt(i):
        return xsb[:, i * D:(i + 1) * D]

    def ln1t(i):
        return ln1[:, i * D:(i + 1) * D]

    @block.sync
    def _(sy):
        SS.inc(sy.dma_start(out=xsb.rearrange("p (t c) -> p t c", c=D),
                            in_=d["x"].rearrange("(t p) c -> p t c", p=128)), 16)
    x_in = SS.val
    if mk.acut == "xload":
        mk.stage_gate = (SS, x_in)
        return

    def r2_fold(v, SVl):
        SVl.inc(v.tensor_mul(S[:, 0:16], S2[:, 0:16], S2[:, 0:16]), 1)
        v.wait_ge(SVl.h, SVl.val)
        return v.tensor_mul(S[:, 0:16], S[:, 0:16], S2[:, 32:48])

    emit_ln_stats(mk, xt, junk1[:, 0:1024], LN_EPS,
                  wait_sems=[(SS, x_in), (SS, mk.atlas_in)], extra=r2_fold)
    if getattr(mk, "_cut_hit", False):
        return

    @block.scalar
    def _(sc):
        sc.wait_ge(SV.h, SV.val)
        SC.inc(sc.activation(S[:, 48:64], S[:, 0:16], AF.Ln, bias=float(LN_EPS)), 1)
        sc.wait_ge(SC.h, SC.val)
        inst = sc.activation(S[:, 16:32], S[:, 48:64], AF.Exp, scale=-0.5)
        SC.inc(inst, 1)

    @block.vector
    def _(v):
        v.wait_ge(SC.h, SC.val)
        v.tensor_mul(S[:, 32:48], S2[:, 0:16], S[:, 16:32])
        inst = v.tensor_mul(S[:, 48:64], S2[:, 16:32], S[:, 16:32])
        SV.inc(inst, 1)
    comb = SV.val
    if mk.acut == "stats":
        @block.vector
        def _(v):
            v.wait_ge(SV.h, comb)
            v.tensor_copy(s["xres"][:, 0:64], S)
            inst = v.tensor_copy(s["xres"][:, 64:112], S2)
            SV.inc(inst, 1)
        mk.stage_gate = (SV, SV.val)
        return

    @block.scalar
    def _(sc):
        sc.wait_ge(SV.h, comb)
        for i in range(NT):
            sc.activation(ln1t(i), xt(i), AF.Identity,
                          bias=S2[:, 16 + i:17 + i], scale=S2[:, i:i + 1])
            inst = sc.activation(R2[:, i * D:(i + 1) * D], xt(i), AF.Identity,
                                 bias=S[:, 48 + i:49 + i], scale=S[:, 32 + i:33 + i])
        SC.inc(inst, 1)
    t_done = SC.val

    @block.sync
    def _(sy):
        sy.wait_ge(SC.h, t_done)
        SS.inc(sy.dma_start(out=d["xspill"].rearrange("(t p) c -> p t c", p=128),
                            in_=xsb.rearrange("p (t c) -> p t c", c=D)), 16)
    x_spilled = SS.val

    if mk.acut == "ln1a":
        emit_dump_bf16(mk, R2, NT * D, (SC, t_done))
        return
    if mk.acut == "ln1z":
        emit_dump_bf16(mk, ln1, NT * D, (SC, t_done))
        return
    aT_in = bounce_transpose(mk, R2.rearrange("p (t c) -> p t c", c=D),
                             lambda cb: R2[:, cb * NTOK:(cb + 1) * NTOK],
                             wait=(SC, t_done))
    if mk.acut == "ln1b":
        @block.vector
        def _(v):
            v.wait_ge(SS.h, aT_in)
            SV.inc(v.engine_nop(), 1)
        emit_dump_bf16(mk, R2, NT * D, (SV, SV.val))
        return

    @block.gpsimd
    def _(g):
        g.wait_ge(SS.h, x_spilled)
        SG.inc(g.dma_start(out=wsl.rearrange("p (cb k) -> p cb k", k=2048),
                           in_=d["wqkv"].rearrange("(cb p) k -> p cb k", p=128)[:, :, 1024:3072]), 16)
        SG.inc(g.dma_start(out=s["rowbuf"][:, 0:2048], in_=mk.rowd("kvbA")), 16)
    wkv_in = SG.val

    @block.vector
    def _(v):
        v.wait_ge(SS.h, aT_in)
        for cb in range(CBN):
            a = R2[:, cb * NTOK:(cb + 1) * NTOK]
            inst = v.tensor_scalar(a, a, mk.col("g1", cb), mk.col("be1", cb), ALU.mult, ALU.add)
        SV.inc(inst, 1)
    aT_rdy = SV.val
    if mk.acut == "ln1":
        emit_dump_bf16(mk, R2, NT * D, (SV, aT_rdy))
        return

    @block.vector
    def _(v):
        v.wait_ge(SS.h, x_spilled)
        v.memset(s["kvacc"], 0.0)
        v.memset(pb[6], 0.0)
        inst = v.memset(pb[7], 0.0)
        for r in range(2):
            off = 2048 + r * 1040
            for h in range(NH):
                inst = v.memset(kfv[:, off + h * 65 + 64: off + h * 65 + 65], 1.0)
        SV.inc(inst, 1)
    ones_rdy = SV.val

    def kf_tile(i):
        return kfv[:, (i % 2) * 1024:(i % 2) * 1024 + 1024]

    def v_tile(i):
        off = 2048 + (i % 2) * 1040
        return kfv[:, off:off + 1040]

    def kv_ps(h):
        grp = h % 2
        return pb[6 + h // 8][64 * grp:64 * grp + 64,
                              ((h % 8) // 2) * 68:((h % 8) // 2) * 68 + 65], grp

    M0, V0, C0 = SM.val, SV.val, SC.val

    @block.tensor
    def _(t):
        t.wait_ge(SV.h, aT_rdy)
        t.wait_ge(SG.h, wkv_in)
        for i in range(NT):
            if i >= 1:
                t.wait_ge(SV.h, V0 + 3 * (i - 1) + 1)
            for ch in range(4):
                ps = pb[ch]
                for cb in range(CBN):
                    t.matmul(ps, R2[:, cb * NTOK + i * 128: cb * NTOK + (i + 1) * 128],
                             wsl[:, cb * 2048 + ch * 512: cb * 2048 + (ch + 1) * 512],
                             start=(cb == 0), stop=False)
                inst = t.matmul(ps, s["ones_c1"], s["rowbuf"][:, ch * 512:(ch + 1) * 512],
                                start=False, stop=True)
            SM.inc(inst, 1)
            t.wait_ge(SV.h, V0 + 3 * i + 2)
            for h in range(0, NH, 2):
                pskv, grp = kv_ps(h)
                inst = t.matmul(pskv, kf_tile(i)[:, h * 64:(h + 1) * 64],
                                v_tile(i)[:, h * 65:(h + 1) * 65],
                                start=True, stop=True, tile_position=(0, 64 * grp))
            SM.inc(inst, 1)
            t.wait_ge(SM.h, SM.val)
            for h in range(1, NH, 2):
                pskv, grp = kv_ps(h)
                inst = t.matmul(pskv, kf_tile(i)[:, h * 64:(h + 1) * 64],
                                v_tile(i)[:, h * 65:(h + 1) * 65],
                                start=True, stop=True, tile_position=(0, 64 * grp))
            SM.inc(inst, 1)

    @block.vector
    def _(v):
        v.wait_ge(SV.h, ones_rdy)
        for i in range(NT):
            v.wait_ge(SM.h, M0 + 3 * i + 1)
            if i >= 2:
                v.wait_ge(SC.h, C0 + i - 1)
            v.tensor_scalar(scr[:, 0:512], pb[0], 0.0, None, ALU.min)
            v.tensor_scalar(scr[:, 512:1024], pb[1], 0.0, None, ALU.min)
            v.tensor_scalar(scr2[:, 0:512], pb[0], 0.0, None, ALU.max)
            v.tensor_scalar(scr2[:, 512:1024], pb[1], 0.0, None, ALU.max)
            v.tensor_copy(v_tile(i).rearrange("p (h c) -> p h c", c=65)[:, 0:8, 0:64],
                          pb[2].rearrange("p (h c) -> p h c", c=64))
            inst = v.tensor_copy(v_tile(i).rearrange("p (h c) -> p h c", c=65)[:, 8:16, 0:64],
                                 pb[3].rearrange("p (h c) -> p h c", c=64))
            SV.inc(inst, 1)
            v.wait_ge(SC.h, C0 + i + 1)
            inst = v.tensor_add(kf_tile(i), scre[:, 0:1024], scr2[:, 0:1024])
            SV.inc(inst, 1)
            v.wait_ge(SM.h, M0 + 3 * i + 3)
            v.tensor_add(s["kvacc"][:, 0:272], s["kvacc"][:, 0:272], pb[6][:, 0:272])
            inst = v.tensor_add(s["kvacc"][:, 272:544], s["kvacc"][:, 272:544], pb[7][:, 0:272])
            SV.inc(inst, 1)

    @block.scalar
    def _(sc):
        for i in range(NT):
            sc.wait_ge(SV.h, V0 + 3 * i + 1)
            inst = sc.activation(scre[:, 0:1024], scr[:, 0:1024], AF.Exp)
            SC.inc(inst, 1)
    pass1_v = V0 + 3 * NT

    @block.vector
    def _(v):
        v.wait_ge(SV.h, pass1_v)
        inst = v.tensor_copy(s["kvsb"], s["kvacc"])
        SV.inc(inst, 1)
    kvsb_rdy = SV.val
    if mk.acut == "pass1":
        emit_dump_bf16(mk, s["kvsb"], 544, (SV, kvsb_rdy))
        return

    @block.gpsimd
    def _(g):
        g.wait_ge(SM.h, M0 + 3 * (NT - 1) + 1)
        SG.inc(g.dma_start(out=wsl[:, 0:8192].rearrange("p (cb k) -> p cb k", k=1024),
                           in_=d["wqkv"].rearrange("(cb p) k -> p cb k", p=128)[:, :, 0:1024]), 16)
        SG.inc(g.dma_start(out=wsl[:, 8192:16384].rearrange("p (cb k) -> p cb k", k=1024),
                           in_=d["woutA"].rearrange("(cb p) k -> p cb k", p=128)), 16)
        SG.inc(g.dma_start(out=s["rowbuf"][:, 0:1024], in_=mk.rowd("outbA")), 16)
    wq_in = SG.val - 32
    wout_in = SG.val
    if mk.acut == "p2load":
        mk.stage_gate = (SG, wout_in)
        return

    M1, V1, C1 = SM.val, SV.val, SC.val
    NP2 = 6 if mk.acut in ("p2mm6", "p2ev6", "p2v6", "p2e6") else 32

    @block.tensor
    def _(t):
        t.wait_ge(SG.h, wq_in)
        t.wait_ge(SV.h, pass1_v)
        for j in range(NP2):
            ob, ch = divmod(j, 4)
            if j >= 6:
                t.wait_ge(SV.h, V1 + 2 * (j - 6) + 2)
            ps = pb[j % 6]
            for cb in range(CBN):
                inst = t.matmul(ps, wsl[:, cb * 1024 + ob * 128: cb * 1024 + (ob + 1) * 128],
                                R2[:, cb * NTOK + ch * 512: cb * NTOK + (ch + 1) * 512],
                                start=(cb == 0), stop=(cb == CBN - 1))
            SM.inc(inst, 1)
    if mk.acut == "p2mm6":
        mk.stage_gate = (SM, M1 + 6)
        return
    if mk.acut in ("p2v6", "p2e6"):
        @block.vector
        def _(v):
            for j in range(NP2):
                ob, ch = divmod(j, 4)
                v.wait_ge(SM.h, M1 + j + 1)
                sl = (j % 2) * 512
                v.tensor_scalar(scr[:, sl:sl + 512], pb[j % 6], mk.col("qbA", ob), 0.0,
                                ALU.add, ALU.min)
                inst = v.tensor_scalar(scr2[:, sl:sl + 512], pb[j % 6], mk.col("qbA", ob), 0.0,
                                       ALU.add, ALU.max)
                SV.inc(inst, 1)
        if mk.acut == "p2v6":
            mk.stage_gate = (SV, V1 + 6)
            return
        @block.scalar
        def _(sc):
            for j in range(NP2):
                sc.wait_ge(SV.h, V1 + j + 1)
                sl = (j % 2) * 512
                inst = sc.activation(scre[:, sl:sl + 512], scr[:, sl:sl + 512], AF.Exp)
                SC.inc(inst, 1)
        mk.stage_gate = (SC, C1 + 6)
        return

    @block.vector
    def _(v):
        for j in range(NP2):
            ob, ch = divmod(j, 4)
            v.wait_ge(SM.h, M1 + j + 1)
            if j >= 2:
                v.wait_ge(SC.h, C1 + j - 1)
            sl = (j % 2) * 512
            v.tensor_scalar(scr[:, sl:sl + 512], pb[j % 6], mk.col("qbA", ob), 0.0,
                            ALU.add, ALU.min)
            inst = v.tensor_scalar(scr2[:, sl:sl + 512], pb[j % 6], mk.col("qbA", ob), 0.0,
                                   ALU.add, ALU.max)
            SV.inc(inst, 1)
            v.wait_ge(SC.h, C1 + j + 1)
            inst = v.tensor_add(qfT[:, ob * NTOK + ch * 512: ob * NTOK + (ch + 1) * 512],
                                scre[:, sl:sl + 512], scr2[:, sl:sl + 512])
            SV.inc(inst, 1)

    @block.scalar
    def _(sc):
        for j in range(NP2):
            sc.wait_ge(SV.h, V1 + 2 * j + 1)
            sl = (j % 2) * 512
            inst = sc.activation(scre[:, sl:sl + 512], scr[:, sl:sl + 512], AF.Exp)
            SC.inc(inst, 1)
    pass2_v = V1 + 2 * NP2
    if mk.acut == "p2ev6":
        mk.stage_gate = (SV, pass2_v)
        return
    if mk.acut == "pass2":
        emit_dump_bf16(mk, qfT, NT * D, (SV, pass2_v))
        return

    M2, V2 = SM.val, SV.val
    NT3 = 2 if mk.acut in ("p3mm2", "p3ev2") else NT

    def y_ps(i, h):
        return pb[(i % 2) * 3 + h // 7][:, (h % 7) * 72:(h % 7) * 72 + 65]

    @block.tensor
    def _(t):
        t.wait_ge(SV.h, kvsb_rdy)
        t.wait_ge(SV.h, pass2_v)
        for i in range(NT3):
            if i >= 2:
                t.wait_ge(SV.h, V2 + 4 * (i - 1))
            for par in range(2):
                for h in range(par, NH, 2):
                    grp = h % 2
                    qs = qfT[64 * grp:64 * grp + 64,
                             (h // 2) * NTOK + i * 128:(h // 2) * NTOK + (i + 1) * 128]
                    ks = s["kvsb"][64 * grp:64 * grp + 64,
                                   (h // 8) * 272 + ((h % 8) // 2) * 68:
                                   (h // 8) * 272 + ((h % 8) // 2) * 68 + 65]
                    inst = t.matmul(y_ps(i, h), qs, ks, start=True, stop=True,
                                    tile_position=(64 * grp, 0))
                SM.inc(inst, 1)
                if par == 0:
                    t.wait_ge(SM.h, SM.val)
    if mk.acut == "p3mm2":
        mk.stage_gate = (SM, M2 + 4)
        return

    @block.vector
    def _(v):
        for i in range(NT3):
            v.wait_ge(SM.h, M2 + 2 * i + 2)
            b0 = (i % 2) * 3
            v.tensor_copy(S[:, 48:55],
                          pb[b0][:, 0:504].rearrange("p (s c) -> p s c", c=72)[:, :, 64:65].squeeze(2))
            v.tensor_copy(S[:, 55:62],
                          pb[b0 + 1][:, 0:504].rearrange("p (s c) -> p s c", c=72)[:, :, 64:65].squeeze(2))
            SV.inc(v.tensor_copy(S[:, 62:64],
                   pb[b0 + 2][:, 0:144].rearrange("p (s c) -> p s c", c=72)[:, :, 64:65].squeeze(2)), 1)
            v.wait_ge(SV.h, SV.val)
            SV.inc(v.tensor_scalar(S[:, 32:48], S[:, 48:64], 1e-6, None, ALU.max), 1)
            v.wait_ge(SV.h, SV.val)
            SV.inc(v.reciprocal(S[:, 48:64], S[:, 32:48]), 1)
            v.wait_ge(SV.h, SV.val)
            for h in range(NH):
                inst = v.tensor_scalar(R2[:, i * D + h * 64: i * D + (h + 1) * 64],
                                       y_ps(i, h)[:, 0:64], S[:, 48 + h:49 + h], None, ALU.mult)
            SV.inc(inst, 1)
    y_done = SV.val
    if mk.acut == "p3ev2":
        mk.stage_gate = (SV, V2 + 8)
        return

    if mk.acut == "y":
        emit_dump_bf16(mk, R2, NT * D, (SV, y_done))
        return
    yT_in = bounce_transpose(mk, R2.rearrange("p (t c) -> p t c", c=D),
                             lambda cb: R2[:, cb * NTOK:(cb + 1) * NTOK],
                             wait=(SV, y_done))

    M3, V3 = SM.val, SV.val

    @block.tensor
    def _(t):
        t.wait_ge(SS.h, yT_in)
        t.wait_ge(SG.h, wout_in)
        for i in range(NT):
            if i >= 3:
                t.wait_ge(SV.h, V3 + (i - 3) + 1)
            for chk in range(2):
                ps = pb[(i % 3) * 2 + chk]
                for cb in range(CBN):
                    t.matmul(ps, R2[:, cb * NTOK + i * 128: cb * NTOK + (i + 1) * 128],
                             wsl[:, 8192 + cb * 1024 + chk * 512:
                                 8192 + cb * 1024 + (chk + 1) * 512],
                             start=(cb == 0), stop=False)
                inst = t.matmul(ps, s["ones_c1"],
                                s["rowbuf"][:, chk * 512:(chk + 1) * 512],
                                start=False, stop=True)
            SM.inc(inst, 1)

    @block.vector
    def _(v):
        for i in range(NT):
            v.wait_ge(SM.h, M3 + i + 1)
            for chk in range(2):
                inst = v.scalar_tensor_tensor(
                    xt(i)[:, chk * 512:(chk + 1) * 512], pb[(i % 3) * 2 + chk], 1.0,
                    ln1t(i)[:, chk * 512:(chk + 1) * 512], ALU.mult, ALU.add)
            SV.inc(inst, 1)
    x1_done = SV.val

    @block.sync
    def _(sy):
        sy.wait_ge(SV.h, x1_done)
        SS.inc(sy.dma_start(out=d["x1spill"].rearrange("(t p) c -> p t c", p=128),
                            in_=xsb.rearrange("p (t c) -> p t c", c=D)), 16)
    x1_spilled = SS.val
    if mk.acut == "pass5":
        mk.stage_gate = (SS, x1_spilled)
        return

    ev = emit_ln_stats(mk, xt, junk2[:, 0:1024], LN_EPS, wait_sems=[(SV, x1_done)])

    a2T_in, t2_dma_done = emit_ln_apply_bounce(
        mk, xt, stg32A, lambda cb: R2[:, cb * NTOK:(cb + 1) * NTOK], ev)
    t2_done = t2_dma_done

    @block.vector
    def _(v):
        v.wait_ge(SS.h, a2T_in)
        for cb in range(CBN):
            a = R2[:, cb * NTOK:(cb + 1) * NTOK]
            inst = v.tensor_scalar(a, a, mk.col("g2", cb), mk.col("be2", cb), ALU.mult, ALU.add)
        SV.inc(inst, 1)
    a2T_rdy = SV.val
    if mk.acut == "ln2":
        mk.stage_gate = (SV, a2T_rdy)
        return

    ph1_last_pred = SM.val + 128

    @block.gpsimd
    def _(g):
        SG.inc(g.dma_start(out=s["rowbuf"][:, 1024:2048], in_=mk.rowd("b2A")), 16)

    w2A_in = SG.val + 128 + 16
    lg_base = w2A_in
    e_load = [lg_base + 64 * (w + 1) for w in range(8)]

    def evac_A(v, w, ntl, ps, chk):
        i = 2 * w + ntl
        dst = xt(i)[:, chk * 512:(chk + 1) * 512]
        xb = xx1[ntl][:, (w % 2) * 1024 + chk * 512:(w % 2) * 1024 + (chk + 1) * 512]
        return v.scalar_tensor_tensor(dst, ps, 1.0, xb, ALU.mult, ALU.add)

    ev_ffn, ph1_last, ph2_last = emit_expert(
        mk, d["w1A"].rearrange("(cb p) f -> p cb f", p=128),
        d["w2A"].rearrange("(fb p) c -> p fb c", p=128),
        "b1A", s["rowbuf"][:, 1024:2048], R2, w1dbl, hstage, w2t, hstream,
        evac_A,
        w1_gate=[(SG, t2_dma_done)], w2_gate=[(SG, w2A_in)], rhs_gate=[(SV, a2T_rdy)],
        evac_wait=lambda w: [(SG, e_load[w])],
    )
    assert ph1_last == ph1_last_pred, (ph1_last, ph1_last_pred)

    @block.gpsimd
    def _(g):
        g.wait_ge(SM.h, ph1_last_pred)
        SG.inc(g.dma_start(out=w2t.rearrange("p (fb c) -> p fb c", c=1024),
                           in_=d["w2A"].rearrange("(fb p) c -> p fb c", p=128)), 16)
        assert SG.val == w2A_in, (SG.val, w2A_in)

    @block.gpsimd
    def _(g):
        assert SG.val == lg_base, (SG.val, lg_base)
        g.wait_ge(SS.h, x1_spilled)
        for w in range(8):
            if w >= 2:
                g.wait_ge(SV.h, mk.e_ev2[w - 2])
            for ntl in range(2):
                i = 2 * w + ntl
                buf = xx1[ntl][:, (w % 2) * 1024:(w % 2) * 1024 + 1024]
                SG.inc(g.dma_start(out=buf, in_=d["x1spill"][i * 128:(i + 1) * 128, :]), 16)
                g.wait_ge(SG.h, SG.val)
                SG.inc(g.dma_start(out=buf, in_=d["xspill"][i * 128:(i + 1) * 128, :],
                                   accum_op=ALU.add), 16)

    mk.stage_gate = (SV, ev_ffn)


def emit_gates(mk, Mr_base, cw_ap, scratch, scale2=None, pb_pair=(0, 1)):
    block, s, pb = mk.block, mk.s, mk.pb
    SC, SV = mk.sem["sc"], mk.sem["v"]
    G8 = s["g8"]
    u = scratch[:, 0:128]
    msk = scratch[:, 128:256]
    w = scratch[:, 256:384]
    expu = scratch[:, 384:512]
    V0 = SV.val
    e_copy = [V0 + i + 1 for i in range(NT)]

    @block.vector
    def _(v):
        for i in range(NT):
            v.wait_ge(mk.sem["mm"].h, Mr_base + i + 1)
            SV.inc(v.tensor_copy(cw_ap[:, i * 8:(i + 1) * 8], pb[pb_pair[i % 2]][:, 0:8]), 1)
        v.wait_ge(SV.h, SV.val)
        SV.inc(v.reduce_max(G8[:, 0:16], cw_ap.rearrange("p (t e) -> p t e", e=8), axis=AX.X), 1)
        v.wait_ge(SV.h, SV.val)
        for i in range(NT):
            inst = v.tensor_scalar(u[:, i * 8:(i + 1) * 8], cw_ap[:, i * 8:(i + 1) * 8],
                                   G8[:, i:i + 1], None, ALU.subtract)
        SV.inc(inst, 1)
        u_rdy = SV.val
        v.wait_ge(SV.h, SV.val)
        SV.inc(v.tensor_scalar(msk[:], u[:], 0.0, None, ALU.is_equal), 1)
        v.wait_ge(SV.h, SV.val)
        SV.inc(v.scalar_tensor_tensor(w[:], msk[:], -1e9, u[:], ALU.mult, ALU.add), 1)
        v.wait_ge(SV.h, SV.val)
        SV.inc(v.reduce_max(G8[:, 16:32], w.rearrange("p (t e) -> p t e", e=8), axis=AX.X), 1)
        v.wait_ge(SV.h, SV.val)
        for i in range(NT):
            inst = v.tensor_scalar(msk[:, i * 8:(i + 1) * 8], u[:, i * 8:(i + 1) * 8],
                                   G8[:, 16 + i:17 + i], None, ALU.subtract)
        SV.inc(inst, 1)
        v.wait_ge(SV.h, SV.val)
        SV.inc(v.tensor_scalar(msk[:], msk[:], 0.0, None, ALU.is_ge), 1)
        mk._gates_u_rdy = u_rdy

    u_rdy = mk._gates_u_rdy
    C0 = SC.val

    @block.scalar
    def _(sc):
        sc.wait_ge(SV.h, u_rdy)
        SC.inc(sc.activation(expu[:], u[:], AF.Exp), 1)

    @block.vector
    def _(v):
        v.wait_ge(SC.h, C0 + 1)
        v.wait_ge(SV.h, SV.val)
        SV.inc(v.tensor_mul(w[:], expu[:], msk[:]), 1)
        v.wait_ge(SV.h, SV.val)
        SV.inc(v.reduce_sum(G8[:, 32:48], w.rearrange("p (t e) -> p t e", e=8), axis=AX.X), 1)
        v.wait_ge(SV.h, SV.val)
        SV.inc(v.reciprocal(G8[:, 0:16], G8[:, 32:48]), 1)
        v.wait_ge(SV.h, SV.val)
        for i in range(NT):
            if scale2 is None:
                inst = v.tensor_scalar(cw_ap[:, i * 8:(i + 1) * 8], w[:, i * 8:(i + 1) * 8],
                                       G8[:, i:i + 1], None, ALU.mult)
            else:
                inst = v.tensor_scalar(cw_ap[:, i * 8:(i + 1) * 8], w[:, i * 8:(i + 1) * 8],
                                       G8[:, i:i + 1], scale2, ALU.mult, ALU.mult)
        SV.inc(inst, 1)
    return e_copy, SV.val



def emit_stage_B(mk):
    nc, block, d, s, pb = mk.nc, mk.block, mk.d, mk.s, mk.pb
    SG, SS, SM, SC, SV = (mk.sem[k] for k in ["g", "sy", "mm", "sc", "v"])
    P = mk.P
    S, S2 = s["stats"], s["stats2"]
    gate = mk.stage_gate

    acc = s["xres"]
    lnBT = mk.sb("lnBT_B", [128, CBN * NTOK], BF16, P + 65536)
    W2Tb = mk.sb("W2T_B", [128, 32 * 1024], BF16, P + 98304)
    w1dbl = mk.sb("w1dbl_B", [128, 2 * 4096], BF16, P + 163840)
    hstage = mk.sb("hstage_B", [128, 2 * 2048], BF16, P + 180224)
    hstream = mk.sb("hstream_B", [128, 2 * 8192], BF16, P + 163840)
    junkB = mk.sb("junk_B", [128, 1024], F32, P + 188416)
    stgB = mk.sb("stg_B", [128, 4 * 1024], F32, P + 163840)
    xrelB = [mk.sb(f"xrel{k}_B", [128, 1024], F32, P + 192512 + k * 4096) for k in range(2)]
    lnBrel = mk.sb("lnBrel_B", [128, NT * D], BF16, P + 98304)
    lnw_bc = mk.sb("lnw_bc", [128, 1024], F32, P + 131072)
    lnb_bc = mk.sb("lnb_bc", [128, 1024], F32, P + 135168)
    rb32 = mk.sb("rb32_B", [1, 2048], F32, P + 139264)

    def xt(i):
        return acc[:, i * D:(i + 1) * D]

    ev = emit_ln_stats(mk, xt, junkB[:, 0:1024], LN_EPS, wait_sems=[gate])

    @block.sync
    def _(sy):
        sy.wait_ge(SV.h, ev)
        SS.inc(sy.dma_start(out=d["xspill"].rearrange("(t p) c -> p t c", p=128),
                            in_=acc.rearrange("p (t c) -> p t c", c=D)), 16)
    xB_spilled = SS.val

    lnBT_in, _sgB = emit_ln_apply_bounce(mk, xt, stgB,
                                        lambda cb: lnBT[:, cb * NTOK:(cb + 1) * NTOK], ev)

    @block.vector
    def _(v):
        v.wait_ge(SS.h, lnBT_in)
        SV.inc(v.engine_nop(), 1)
    lnBT_rdy = SV.val

    @block.gpsimd
    def _(g):
        g.wait_ge(SS.h, xB_spilled)
        SG.inc(g.dma_start(out=s["wr8"].rearrange("p (cb r) -> p cb r", r=8),
                           in_=d["WrT"].rearrange("(cb p) r -> p cb r", p=128)), 16)
        SG.inc(g.dma_start(out=s["rowbuf"][:, 2048:2056], in_=mk.rowd("brB")), 16)
    wr_in = SG.val

    Mr = SM.val
    e_gcopy_pred = [SV.val + i + 1 for i in range(NT)]

    @block.tensor
    def _(t):
        t.wait_ge(SG.h, wr_in)
        t.wait_ge(SV.h, lnBT_rdy)
        for i in range(NT):
            if i >= 2:
                t.wait_ge(SV.h, e_gcopy_pred[i - 2])
            ps = pb[i % 2][:, 0:8]
            for cb in range(CBN):
                t.matmul(ps, lnBT[:, cb * NTOK + i * 128: cb * NTOK + (i + 1) * 128],
                         s["wr8"][:, cb * 8:(cb + 1) * 8], start=(cb == 0), stop=False)
            inst = t.matmul(ps, s["ones_c1"], s["rowbuf"][:, 2048:2056],
                            start=False, stop=True)
            SM.inc(inst, 1)

    e_gcopy, cw_done = emit_gates(mk, Mr, s["cw"], junkB[:, 0:512])
    assert e_gcopy == e_gcopy_pred

    ph2_prev = None
    ev_e = None
    for e in range(8):
        roff = (e % 2) * 1024

        @block.gpsimd
        def _(g, e=e, roff=roff, ph2_prev=ph2_prev):
            if e == 0:
                g.wait_ge(gate[0].h, gate[1])
            else:
                g.wait_ge(SM.h, ph2_prev)
            SG.inc(g.dma_start(out=s["rowbuf"][:, roff:roff + 1024],
                               in_=mk.rowd(f"b2B{e}")), 16)
            SG.inc(g.dma_start(out=W2Tb.rearrange("p (fb c) -> p fb c", c=1024),
                               in_=d["W2T"][e].rearrange("(fb p) c -> p fb c", p=128)), 16)
        w2_in = SG.val

        def evac_B(v, w, ntl, ps, chk, e=e):
            i = 2 * w + ntl
            dst = acc[:, i * D + chk * 512: i * D + (chk + 1) * 512]
            gcol = s["cw"][:, i * 8 + e: i * 8 + e + 1]
            if e == 0:
                return v.tensor_scalar(dst, ps, gcol, None, ALU.mult)
            return v.scalar_tensor_tensor(dst, ps, gcol, dst, ALU.mult, ALU.add)

        w1g = [(SV, cw_done)] if e == 0 else [(SM, ph2_prev)]
        ev_e, ph1_last, ph2_last = emit_expert(
            mk, d["W1T"][e].rearrange("(cb p) f -> p cb f", p=128),
            d["W2T"][e].rearrange("(fb p) c -> p fb c", p=128),
            f"b1B{e}", s["rowbuf"][:, roff:roff + 1024],
            lnBT, w1dbl, hstage, W2Tb, hstream, evac_B,
            w1_gate=w1g, w2_gate=[(SG, w2_in)],
            rhs_gate=[(SV, cw_done)] if e == 0 else [],
        )
        ph2_prev = ph2_last
    experts_done = ev_e

    @block.gpsimd
    def _(g):
        g.wait_ge(SM.h, ph2_prev)
        SG.inc(g.dma_start(out=lnBrel.rearrange("p (t c) -> p t c", c=D),
                           in_=d["tb"].rearrange("(t p) c -> p t c", p=128)), 16)
        SG.inc(g.dma_start(out=rb32[:, 0:1024], in_=mk.rowd("lnwB")), 16)
        SG.inc(g.dma_start(out=rb32[:, 1024:2048], in_=mk.rowd("lnbB")), 16)
    lnBrel_in = SG.val

    @block.tensor
    def _(t):
        t.wait_ge(SG.h, lnBrel_in)
        t.wait_ge(SV.h, experts_done)
        t.matmul(pb[0], s["ones_c1f"], rb32[:, 0:512], start=True, stop=True)
        t.matmul(pb[1], s["ones_c1f"], rb32[:, 512:1024], start=True, stop=True)
        t.matmul(pb[2], s["ones_c1f"], rb32[:, 1024:1536], start=True, stop=True)
        inst = t.matmul(pb[3], s["ones_c1f"], rb32[:, 1536:2048], start=True, stop=True)
        SM.inc(inst, 1)
    bc_mm = SM.val

    @block.scalar
    def _(sc):
        sc.wait_ge(SM.h, bc_mm)
        sc.copy(lnw_bc[:, 0:512], pb[0])
        sc.copy(lnw_bc[:, 512:1024], pb[1])
        sc.copy(lnb_bc[:, 0:512], pb[2])
        inst = sc.copy(lnb_bc[:, 512:1024], pb[3])
        SC.inc(inst, 1)
    bc_done = SC.val

    @block.vector
    def _(v):
        v.wait_ge(SV.h, experts_done)
        v.wait_ge(SG.h, lnBrel_in)
        inst = None
        for i in range(NT):
            inst = v.tensor_add(xt(i), xt(i), lnBrel[:, i * D:(i + 1) * D])
        SV.inc(inst, 1)
    z_done = SV.val

    ev2 = emit_ln_stats(mk, xt, junkB[:, 0:1024], B_EPS, wait_sems=[(SV, z_done)])

    lgB = SG.val
    e_xrelB = [lgB + 16 * (i + 1) for i in range(NT)]
    CF = SC.val
    VF = SV.val
    e_fin = [VF + 3 * i + 3 for i in range(NT)]

    @block.gpsimd
    def _(g):
        g.wait_ge(SV.h, z_done)
        for i in range(NT):
            if i >= 2:
                g.wait_ge(SV.h, e_fin[i - 2])
            SG.inc(g.dma_start(out=xrelB[i % 2], in_=d["xspill"][i * 128:(i + 1) * 128, :]), 16)

    @block.scalar
    def _(sc):
        sc.wait_ge(SV.h, ev2)
        sc.wait_ge(SC.h, bc_done)
        for i in range(NT):
            if i >= 1:
                sc.wait_ge(SV.h, e_fin[i - 1])
            inst = sc.activation(junkB[:, 0:1024], xt(i), AF.Identity,
                                 bias=S2[:, 16 + i:17 + i], scale=S2[:, i:i + 1])
            SC.inc(inst, 1)

    @block.vector
    def _(v):
        for i in range(NT):
            v.wait_ge(SC.h, CF + i + 1)
            v.wait_ge(SG.h, e_xrelB[i])
            SV.inc(v.tensor_mul(junkB[:, 0:1024], junkB[:, 0:1024], lnw_bc[:, 0:1024]), 1)
            v.wait_ge(SV.h, SV.val)
            SV.inc(v.tensor_add(junkB[:, 0:1024], junkB[:, 0:1024], lnb_bc[:, 0:1024]), 1)
            v.wait_ge(SV.h, SV.val)
            inst = v.tensor_add(xt(i), junkB[:, 0:1024], xrelB[i % 2][:, 0:1024])
            SV.inc(inst, 1)
    assert SV.val == e_fin[-1]
    mk.stage_gate = (SV, SV.val)



def emit_stage_C(mk):
    nc, block, d, s, pb = mk.nc, mk.block, mk.d, mk.s, mk.pb
    SG, SS, SM, SC, SV = (mk.sem[k] for k in ["g", "sy", "mm", "sc", "v"])
    P = mk.P
    S, S2 = s["stats"], s["stats2"]
    gate = mk.stage_gate
    NK = 1024

    xsb = s["xres"]
    A1 = mk.sb("A1_C", [128, CBN * NTOK], BF16, P + 65536)
    xdT = mk.sb("xdT_C", [128, CBN * NK], BF16, P + 98304)
    pbuf = mk.sb("p_C", [128, 8 * 512], BF16, P + 98304)
    ytmp = mk.sb("ytmp_C", [128, 512], BF16, P + 106496)
    screc = mk.sb("screc_C", [128, 1024], F32, P + 107520)
    qTC = mk.sb("qT_C", [128, CBN * NTOK], BF16, P + 114688)
    kTC = mk.sb("kT_C", [128, CBN * NK], BF16, P + 147456)
    vC = mk.sb("v_C", [128, 8 * 1040], BF16, P + 163840)
    stgC = mk.sb("stg_C", [128, 4 * 1024], F32, P + 163840)
    wslC = mk.sb("wsl_C", [128, CBN * 1024], BF16, P + 180608)
    junkC = mk.sb("junk_C", [128, 1024], F32, P + 196992)

    def xt(i):
        return xsb[:, i * D:(i + 1) * D]

    ev = emit_ln_stats(mk, xt, junkC[:, 0:1024], LN_EPS, wait_sems=[gate])
    lnCT_in, _sgC = emit_ln_apply_bounce(mk, xt, stgC,
                                        lambda cb: A1[:, cb * NTOK:(cb + 1) * NTOK], ev)

    @block.vector
    def _(v):
        v.wait_ge(SS.h, lnCT_in)
        for cb in range(CBN):
            src = A1[:, cb * NTOK:(cb + 1) * NTOK].rearrange("p (j s) -> p j s", s=2)
            dst = xdT[:, cb * NK:(cb + 1) * NK]
            inst = v.tensor_scalar(dst, src[:, :, 0:1].squeeze(2),
                                   mk.col("cw1", cb), mk.col("cbb", cb), ALU.mult, ALU.add)
        SV.inc(inst, 1)
        v.wait_ge(SV.h, SV.val)
        for cb in range(CBN):
            src = A1[:, cb * NTOK:(cb + 1) * NTOK].rearrange("p (j s) -> p j s", s=2)
            dst = xdT[:, cb * NK:(cb + 1) * NK]
            inst = v.scalar_tensor_tensor(dst, src[:, :, 1:2].squeeze(2), mk.col("cw2", cb),
                                          dst, ALU.mult, ALU.add)
        SV.inc(inst, 1)
        v.wait_ge(SV.h, SV.val)
        for cb in range(CBN):
            src = A1[:, cb * NTOK:(cb + 1) * NTOK].rearrange("p (j s) -> p j s", s=2)
            dst = xdT[:, cb * NK:(cb + 1) * NK]
            inst = v.scalar_tensor_tensor(dst[:, 1:1024], src[:, 0:1023, 1:2].squeeze(2),
                                          mk.col("cw0", cb), dst[:, 1:1024], ALU.mult, ALU.add)
        SV.inc(inst, 1)
    xdT_rdy = SV.val

    @block.gpsimd
    def _(g):
        g.wait_ge(gate[0].h, gate[1])
        SG.inc(g.dma_start(out=wslC.rearrange("p (cb k) -> p cb k", k=1024),
                           in_=d["qwT"].rearrange("(cb p) k -> p cb k", p=128)), 16)
        SG.inc(g.dma_start(out=s["rowbuf"][:, 0:1024], in_=mk.rowd("vbC")), 16)
        SG.inc(g.dma_start(out=s["rowbuf"][:, 1024:2048], in_=mk.rowd("outbC")), 16)
    qw_in = SG.val

    def gemm_feat(dst, rhs_src, bias_name, nch, src_len, waits):
        M0, C0 = SM.val, SC.val

        @block.tensor
        def _(t):
            for se, val in waits:
                t.wait_ge(se.h, val)
            for j in range(CBN * nch):
                ob, ch = divmod(j, nch)
                if j >= 6:
                    t.wait_ge(SC.h, C0 + (j - 6) + 1)
                ps = pb[j % 6]
                for cb in range(CBN):
                    inst = t.matmul(ps, wslC[:, cb * 1024 + ob * 128: cb * 1024 + (ob + 1) * 128],
                                    rhs_src[:, cb * src_len + ch * 512: cb * src_len + (ch + 1) * 512],
                                    start=(cb == 0), stop=(cb == CBN - 1))
                SM.inc(inst, 1)

        @block.scalar
        def _(sc):
            for j in range(CBN * nch):
                ob, ch = divmod(j, nch)
                sc.wait_ge(SM.h, M0 + j + 1)
                inst = sc.activation(dst[:, ob * src_len + ch * 512:
                                         ob * src_len + (ch + 1) * 512],
                                     pb[j % 6], AF.Identity, bias=mk.col(bias_name, ob))
                SC.inc(inst, 1)
        return SM.val, SC.val

    Mq_end, q_done = gemm_feat(qTC, A1, "qbC", 4, NTOK, [(SG, qw_in), (SS, lnCT_in)])

    @block.gpsimd
    def _(g):
        g.wait_ge(SM.h, Mq_end)
        SG.inc(g.dma_start(out=wslC.rearrange("p (cb k) -> p cb k", k=1024),
                           in_=d["kwT"].rearrange("(cb p) k -> p cb k", p=128)), 16)
    kw_in = SG.val

    Mk_end, k_done = gemm_feat(kTC, xdT, "kbC", 2, NK, [(SG, kw_in), (SV, xdT_rdy), (SC, q_done)])

    @block.gpsimd
    def _(g):
        g.wait_ge(SM.h, Mk_end)
        SG.inc(g.dma_start(out=wslC.rearrange("p (cb k) -> p cb k", k=1024),
                           in_=d["vwT"].rearrange("(cb p) k -> p cb k", p=128)), 16)
    vw_in = SG.val

    @block.vector
    def _(v):
        v.wait_ge(SC.h, k_done)
        for mt in range(8):
            for h in range(NH):
                inst = v.memset(vC[:, mt * 1040 + h * 65 + 64: mt * 1040 + h * 65 + 65], 1.0)
        SV.inc(inst, 1)
    vones = SV.val

    Mv, Vv = SM.val, SV.val

    @block.tensor
    def _(t):
        t.wait_ge(SG.h, vw_in)
        t.wait_ge(SV.h, vones)
        t.wait_ge(SC.h, k_done)
        for j in range(16):
            mt, chk = divmod(j, 2)
            if j >= 6:
                t.wait_ge(SV.h, Vv + (j - 6) + 1)
            ps = pb[j % 6]
            for cb in range(CBN):
                t.matmul(ps, xdT[:, cb * NK + mt * 128: cb * NK + (mt + 1) * 128],
                         wslC[:, cb * 1024 + chk * 512: cb * 1024 + (chk + 1) * 512],
                         start=(cb == 0), stop=False)
            inst = t.matmul(ps, s["ones_c1"], s["rowbuf"][:, chk * 512:(chk + 1) * 512],
                            start=False, stop=True)
            SM.inc(inst, 1)

    @block.vector
    def _(v):
        for j in range(16):
            mt, chk = divmod(j, 2)
            v.wait_ge(SM.h, Mv + j + 1)
            inst = v.tensor_copy(
                vC[:, mt * 1040:(mt + 1) * 1040]
                .rearrange("p (h c) -> p h c", c=65)[:, chk * 8:(chk + 1) * 8, 0:64],
                pb[j % 6].rearrange("p (h c) -> p h c", c=64))
            SV.inc(inst, 1)
    v_done = SV.val

    @block.gpsimd
    def _(g):
        g.wait_ge(SM.h, Mv + 16)
        SG.inc(g.dma_start(out=wslC.rearrange("p (cb k) -> p cb k", k=1024),
                           in_=d["owT"].rearrange("(cb p) k -> p cb k", p=128)), 16)
    ow_in = SG.val

    jobs = [(h, ch) for h in range(NH) for ch in range(4)]
    Ma, Va, Ca = SM.val, SV.val, SC.val
    e_qk, e_av, e_rb, e_shift = {}, {}, {}, {}
    cnt = Ma
    for (h, ch) in jobs:
        for mb in range(8):
            cnt += 1
            e_qk[(h, ch, mb)] = cnt
        cnt += 1
        e_av[(h, ch)] = cnt
        cnt += 1
        e_rb[(h, ch)] = cnt
        if h % 2 == 1:
            cnt += 1
            e_shift[(h, ch)] = cnt
    e_exp = {}
    cnt = Ca
    for (h, ch) in jobs:
        for mb in range(8):
            cnt += 1
            e_exp[(h, ch, mb)] = cnt
    e_max, e_rec, e_rbc, e_ymul, e_fin = {}, {}, {}, {}, {}
    cnt = Va
    for (h, ch) in jobs:
        cnt += 1
        e_max[(h, ch)] = cnt
        cnt += 1
        e_rec[(h, ch)] = cnt
        cnt += 1
        e_rbc[(h, ch)] = cnt
        cnt += 1
        e_ymul[(h, ch)] = cnt
        if h % 2 == 1:
            cnt += 1
            e_fin[(h, ch)] = cnt
        else:
            e_fin[(h, ch)] = cnt
    v_att_end = cnt

    def prev_job(idx):
        return jobs[idx - 1] if idx >= 1 else None

    @block.tensor
    def _(t):
        t.wait_ge(SC.h, q_done)
        t.wait_ge(SV.h, v_done)
        for jidx, (h, ch) in enumerate(jobs):
            grp = h % 2
            for mb in range(8):
                gq = jidx * 8 + mb
                if gq >= 2:
                    ph, pc, pm = jobs[(gq - 2) // 8][0], jobs[(gq - 2) // 8][1], (gq - 2) % 8
                    t.wait_ge(SC.h, e_exp[(ph, pc, pm)])
                lh = kTC[64 * grp:64 * grp + 64,
                         (h // 2) * NK + mb * 128:(h // 2) * NK + (mb + 1) * 128]
                rh = qTC[64 * grp:64 * grp + 64,
                         (h // 2) * NTOK + ch * 512:(h // 2) * NTOK + (ch + 1) * 512]
                inst = t.matmul(pb[mb % 2], lh, rh, start=True, stop=True,
                                tile_position=(64 * grp, 0))
                SM.inc(inst, 1)
            pj = prev_job(jidx)
            if pj is not None:
                t.wait_ge(SV.h, e_ymul[pj])
            for mb in range(8):
                t.wait_ge(SC.h, e_exp[(h, ch, mb)])
                inst = t.matmul(pb[2][0:65, 0:512],
                                vC[:, mb * 1040 + h * 65: mb * 1040 + h * 65 + 65],
                                pbuf[:, mb * 512:(mb + 1) * 512],
                                start=(mb == 0), stop=(mb == 7))
            SM.inc(inst, 1)
            t.wait_ge(SV.h, e_rec[(h, ch)])
            inst = t.matmul(pb[4][0:64, 0:512], s["onep64"][64:65, :],
                            screc[64:65, 0:512], start=True, stop=True,
                            tile_position=(64, 0))
            SM.inc(inst, 1)
            if h % 2 == 1:
                t.wait_ge(SV.h, e_ymul[(h, ch)])
                inst = t.matmul(pb[3], s["ishift"][0:64, :], ytmp[0:64, 0:512],
                                start=True, stop=True)
                SM.inc(inst, 1)

    @block.scalar
    def _(sc):
        for jidx, (h, ch) in enumerate(jobs):
            for mb in range(8):
                sc.wait_ge(SM.h, e_qk[(h, ch, mb)])
                if jidx >= 1:
                    sc.wait_ge(SM.h, e_av[prev_job(jidx)])
                inst = sc.activation(pbuf[:, mb * 512:(mb + 1) * 512],
                                     pb[mb % 2], AF.Exp, scale=0.125)
                SC.inc(inst, 1)

    @block.vector
    def _(v):
        for jidx, (h, ch) in enumerate(jobs):
            grp = h % 2
            v.wait_ge(SM.h, e_av[(h, ch)])
            if jidx >= 1:
                v.wait_ge(SM.h, e_rb[prev_job(jidx)])
            inst = v.tensor_scalar(screc[64:65, 512:1024], pb[2][64:65, 0:512],
                                   1e-38, None, ALU.max)
            SV.inc(inst, 1)
            v.wait_ge(SV.h, e_max[(h, ch)])
            inst = v.reciprocal(screc[64:65, 0:512], screc[64:65, 512:1024])
            SV.inc(inst, 1)
            v.wait_ge(SM.h, e_rb[(h, ch)])
            inst = v.tensor_copy(screc[0:64, 512:1024], pb[4][0:64, 0:512])
            SV.inc(inst, 1)
            v.wait_ge(SV.h, e_rbc[(h, ch)])
            if grp == 0:
                dstY = A1[0:64, (h // 2) * NTOK + ch * 512:(h // 2) * NTOK + (ch + 1) * 512]
                inst = v.tensor_tensor(dstY, pb[2][0:64, 0:512], screc[0:64, 512:1024], ALU.mult)
                SV.inc(inst, 1)
            else:
                inst = v.tensor_tensor(ytmp[0:64, 0:512], pb[2][0:64, 0:512],
                                       screc[0:64, 512:1024], ALU.mult)
                SV.inc(inst, 1)
                v.wait_ge(SM.h, e_shift[(h, ch)])
                dstY = A1[64:128, (h // 2) * NTOK + ch * 512:(h // 2) * NTOK + (ch + 1) * 512]
                inst = v.tensor_copy(dstY, pb[3][64:128, 0:512])
                SV.inc(inst, 1)
    assert SV.val == v_att_end

    M4, V4 = SM.val, SV.val

    @block.tensor
    def _(t):
        t.wait_ge(SV.h, v_att_end)
        t.wait_ge(SG.h, ow_in)
        for i in range(NT):
            if i >= 3:
                t.wait_ge(SV.h, V4 + (i - 3) + 1)
            for chk in range(2):
                ps = pb[(i % 3) * 2 + chk]
                for cb in range(CBN):
                    t.matmul(ps, A1[:, cb * NTOK + i * 128: cb * NTOK + (i + 1) * 128],
                             wslC[:, cb * 1024 + chk * 512: cb * 1024 + (chk + 1) * 512],
                             start=(cb == 0), stop=False)
                inst = t.matmul(ps, s["ones_c1"],
                                s["rowbuf"][:, 1024 + chk * 512: 1024 + (chk + 1) * 512],
                                start=False, stop=True)
            SM.inc(inst, 1)

    @block.vector
    def _(v):
        for i in range(NT):
            v.wait_ge(SM.h, M4 + i + 1)
            for chk in range(2):
                dst = xt(i)[:, chk * 512:(chk + 1) * 512]
                inst = v.scalar_tensor_tensor(dst, pb[(i % 3) * 2 + chk], 1.0, dst,
                                              ALU.mult, ALU.add)
            SV.inc(inst, 1)
    mk.stage_gate = (SV, SV.val)



def emit_stage_D(mk):
    nc, block, d, s, pb = mk.nc, mk.block, mk.d, mk.s, mk.pb
    SG, SS, SM, SC, SV = (mk.sem[k] for k in ["g", "sy", "mm", "sc", "v"])
    P = mk.P
    S, S2 = s["stats"], s["stats2"]
    gate = mk.stage_gate

    xsb = s["xres"]
    lnDT = mk.sb("lnDT_D", [128, CBN * NTOK], BF16, P + 65536)
    wslD = mk.sb("wsl_D", [128, CBN * 1024], BF16, P + 98304)
    stgD = mk.sb("stg_D", [128, 4 * 1024], F32, P + 98304)
    AcatS = mk.sb("Acat_D", [128, CBN * 128], BF16, P + 114688)
    BcatS = mk.sb("Bcat_D", [128, 1024], BF16, P + 116736)
    axS = mk.sb("ax_D", [128, NT * 128], F32, P + 118784)
    axwS = mk.sb("axw_D", [128, NT * 128], BF16, P + 126976)
    axwT = mk.sb("axwT_D", [128, NTOK], BF16, P + 131072)
    junkD = mk.sb("junk_D", [128, 1024], F32, P + 135168)

    def xt(i):
        return xsb[:, i * D:(i + 1) * D]

    ev = emit_ln_stats(mk, xt, junkD[:, 0:1024], LN_EPS, wait_sems=[gate])
    lnDT_in, _sgD = emit_ln_apply_bounce(mk, xt, stgD,
                                        lambda cb: lnDT[:, cb * NTOK:(cb + 1) * NTOK], ev)

    @block.gpsimd
    def _(g):
        g.wait_ge(SS.h, lnDT_in)
        SG.inc(g.dma_start(out=wslD.rearrange("p (cb k) -> p cb k", k=1024),
                           in_=d["bwT"].rearrange("(cb p) k -> p cb k", p=128)), 16)
        SG.inc(g.dma_start(out=AcatS.rearrange("p (cb r) -> p cb r", r=128),
                           in_=d["Acat"].rearrange("(cb p) r -> p cb r", p=128)), 16)
        SG.inc(g.dma_start(out=BcatS, in_=d["Bcat"]), 16)
        SG.inc(g.dma_start(out=s["wr8"].rearrange("p (cb r) -> p cb r", r=8),
                           in_=d["rwT"].rearrange("(cb p) r -> p cb r", p=128)), 16)
        SG.inc(g.dma_start(out=s["rowbuf"][:, 0:1024], in_=mk.rowd("bbD")), 16)
        SG.inc(g.dma_start(out=s["rowbuf"][:, 1024:1032], in_=mk.rowd("brD")), 16)
    ld_in = SG.val

    Mr = SM.val
    e_gcopy_pred = [SV.val + i + 1 for i in range(NT)]

    @block.tensor
    def _(t):
        t.wait_ge(SG.h, ld_in)
        t.wait_ge(SS.h, lnDT_in)
        for i in range(NT):
            if i >= 2:
                t.wait_ge(SV.h, e_gcopy_pred[i - 2])
            ps = pb[i % 2][:, 0:8]
            for cb in range(CBN):
                t.matmul(ps, lnDT[:, cb * NTOK + i * 128: cb * NTOK + (i + 1) * 128],
                         s["wr8"][:, cb * 8:(cb + 1) * 8], start=(cb == 0), stop=False)
            inst = t.matmul(ps, s["ones_c1"], s["rowbuf"][:, 1024:1032],
                            start=False, stop=True)
            SM.inc(inst, 1)

    e_gcopy, cwD_done = emit_gates(mk, Mr, s["cwD"], junkD[:, 0:512], scale2=2.0)
    assert e_gcopy == e_gcopy_pred

    Ma = SM.val
    e_ax = [Ma + i + 1 for i in range(NT)]
    Vax = SV.val
    e_axev = [Vax + 2 * i + 2 for i in range(NT)]

    @block.tensor
    def _(t):
        for i in range(NT):
            if i >= 2:
                t.wait_ge(SV.h, e_axev[i - 2])
            ps2 = pb[2 + i % 2]
            for cb in range(CBN):
                inst = t.matmul(ps2[:, 0:128],
                                lnDT[:, cb * NTOK + i * 128: cb * NTOK + (i + 1) * 128],
                                AcatS[:, cb * 128:(cb + 1) * 128],
                                start=(cb == 0), stop=(cb == CBN - 1))
            SM.inc(inst, 1)

    @block.vector
    def _(v):
        v.wait_ge(SV.h, cwD_done)
        for i in range(NT):
            v.wait_ge(SM.h, e_ax[i])
            SV.inc(v.tensor_copy(axS[:, i * 128:(i + 1) * 128], pb[2 + i % 2][:, 0:128]), 1)
            v.wait_ge(SV.h, SV.val)
            for e in range(8):
                inst = v.tensor_scalar(axwS[:, i * 128 + e * 16: i * 128 + (e + 1) * 16],
                                       axS[:, i * 128 + e * 16: i * 128 + (e + 1) * 16],
                                       s["cwD"][:, i * 8 + e: i * 8 + e + 1], None, ALU.mult)
            SV.inc(inst, 1)
    axw_done = SV.val

    axwT_in = bounce_transpose(mk, axwS.rearrange("p (t r) -> p t r", r=128),
                               lambda cb: axwT[:, 0:NTOK],
                               wait=(SV, axw_done), ncols=128)

    M5, V5 = SM.val, SV.val

    @block.tensor
    def _(t):
        t.wait_ge(SS.h, axwT_in)
        for i in range(NT):
            if i >= 3:
                t.wait_ge(SV.h, V5 + (i - 3) + 1)
            for chk in range(2):
                ps = pb[(i % 3) * 2 + chk]
                for cb in range(CBN):
                    t.matmul(ps, lnDT[:, cb * NTOK + i * 128: cb * NTOK + (i + 1) * 128],
                             wslD[:, cb * 1024 + chk * 512: cb * 1024 + (chk + 1) * 512],
                             start=(cb == 0), stop=False)
                t.matmul(ps, s["ones_c1"], s["rowbuf"][:, chk * 512:(chk + 1) * 512],
                         start=False, stop=False)
                inst = t.matmul(ps, axwT[:, i * 128:(i + 1) * 128],
                                BcatS[:, chk * 512:(chk + 1) * 512],
                                start=False, stop=True)
            SM.inc(inst, 1)

    @block.vector
    def _(v):
        for i in range(NT):
            v.wait_ge(SM.h, M5 + i + 1)
            for chk in range(2):
                dst = xt(i)[:, chk * 512:(chk + 1) * 512]
                inst = v.scalar_tensor_tensor(dst, pb[(i % 3) * 2 + chk], 1.0, dst,
                                              ALU.mult, ALU.add)
            SV.inc(inst, 1)
    mk.stage_gate = (SV, SV.val)


def emit_finish(mk):
    block, d = mk.block, mk.d
    SS = mk.sem["sy"]
    gate = mk.stage_gate
    fin = mk.s["xres"]

    @block.sync
    def _(sy):
        sy.wait_ge(gate[0].h, gate[1])
        SS.inc(sy.dma_start(out=d["out"].rearrange("(t p) c -> p t c", p=128),
                            in_=fin.rearrange("p (t c) -> p t c", c=D)), 16)
        sy.wait_ge(SS.h, SS.val)



def run(inputs, stages=("A", "B", "C", "D"), cores=8):
    c, xs, colmap, rowmap = host_prepare(inputs)
    nc, mk = build_program(colmap, rowmap, stages)
    in_maps = []
    for b in range(cores):
        m = dict(c)
        m["x"] = xs[b]
        in_maps.append(m)
    res = run_bass_kernel_spmd(nc, in_maps, core_ids=list(range(cores)))
    out = np.stack([res.results[b]["out"] for b in range(cores)], axis=0)
    return out


def kernel(**inputs):
    c, xs, colmap, rowmap = host_prepare(inputs)
    nc, mk = build_program(colmap, rowmap, ("A", "B", "C", "D"))
    in_maps = []
    for b in range(8):
        m = dict(c)
        m["x"] = xs[b]
        in_maps.append(m)
    res = run_bass_kernel_spmd(nc, in_maps, core_ids=list(range(8)))
    return np.stack([res.results[b]["out"] for b in range(8)], axis=0)


# version 02
# speedup vs baseline: 1.0832x; 1.0832x over previous
import os
import numpy as np
import concourse.bass as bass
import concourse.mybir as mybir
from concourse.bass_utils import run_bass_kernel_spmd

F32 = mybir.dt.float32
BF16 = mybir.dt.bfloat16
AF = mybir.ActivationFunctionType
ALU = mybir.AluOpType
AX = mybir.AxisListType

NTOK, D, NT, CBN, NH, DFF = 2048, 1024, 16, 8, 16, 4096
LN_EPS, B_EPS = 1e-5, 1e-12
SB_LIMIT = 212500



def _colmat(v):
    return np.ascontiguousarray(np.asarray(v, np.float32).reshape(-1, 128).T)


def host_prepare(inp):
    f32 = lambda k: np.asarray(inp[k], np.float32)
    c = {}
    c["wqkv"] = np.ascontiguousarray(f32("L0_attn_qkv_w"))
    c["woutA"] = np.ascontiguousarray(f32("L0_attn_out_w"))
    c["w1A"] = np.ascontiguousarray(f32("L0_attn_w1"))
    c["w2A"] = np.ascontiguousarray(f32("L0_attn_w2"))
    c["W1T"] = np.ascontiguousarray(np.transpose(f32("L0_ffn_W1"), (0, 2, 1)))
    c["W2T"] = np.ascontiguousarray(np.transpose(f32("L0_ffn_W2"), (0, 2, 1)))
    c["WrT"] = np.ascontiguousarray(f32("L0_ffn_Wr").T)
    c["qwT"] = np.ascontiguousarray(f32("L1_attn_q_w").T)
    c["kwT"] = np.ascontiguousarray(f32("L1_attn_k_w").T)
    c["vwT"] = np.ascontiguousarray(f32("L1_attn_v_w").T)
    c["owT"] = np.ascontiguousarray(f32("L1_attn_out_w").T)
    c["bwT"] = np.ascontiguousarray(f32("L1_ffn_base_w").T)
    c["Acat"] = np.ascontiguousarray(np.transpose(f32("L1_ffn_lora_A"), (1, 0, 2)).reshape(D, 128))
    c["Bcat"] = np.ascontiguousarray(f32("L1_ffn_lora_B").reshape(128, D))
    c["rwT"] = np.ascontiguousarray(f32("L1_ffn_router_w").T)
    c["ident"] = np.eye(128, dtype=np.float32)

    cols, colmap = [], {}

    def addcol(name, v):
        m = _colmat(v)
        colmap[name] = (sum(x.shape[1] for x in cols), m.shape[1])
        cols.append(m)

    addcol("g1", inp["L0_attn_g1"]); addcol("be1", inp["L0_attn_be1"])
    addcol("g2", inp["L0_attn_g2"]); addcol("be2", inp["L0_attn_be2"])
    addcol("qbA", f32("L0_attn_qkv_b")[0:1024])
    addcol("b1A", inp["L0_attn_b1"])
    for e in range(8):
        addcol(f"b1B{e}", f32("L0_ffn_b1")[e])
    addcol("qbC", inp["L1_attn_q_b"]); addcol("kbC", inp["L1_attn_k_b"])
    cw3 = f32("L1_attn_conv_w").reshape(D, 3)
    addcol("cw0", cw3[:, 0]); addcol("cw1", cw3[:, 1]); addcol("cw2", cw3[:, 2])
    addcol("cbb", inp["L1_attn_conv_b"])
    c["colatlas"] = np.concatenate(cols, axis=1).astype(np.float32)

    rows, rowmap = [], {}

    def addrow(name, v):
        v = np.asarray(v, np.float32).reshape(1, -1)
        rowmap[name] = (sum(x.shape[1] for x in rows), v.shape[1])
        rows.append(v)

    addrow("kvbA", f32("L0_attn_qkv_b")[1024:3072])
    addrow("b2A", inp["L0_attn_b2"]); addrow("outbA", inp["L0_attn_out_b"])
    addrow("brB", inp["L0_ffn_br"])
    for e in range(8):
        addrow(f"b2B{e}", f32("L0_ffn_b2")[e])
    addrow("lnwB", inp["L0_ffn_ln_w"]); addrow("lnbB", inp["L0_ffn_ln_b"])
    addrow("vbC", inp["L1_attn_v_b"]); addrow("outbC", inp["L1_attn_out_b"])
    addrow("bbD", inp["L1_ffn_base_b"]); addrow("brD", inp["L1_ffn_router_b"])
    c["rowatlas"] = np.concatenate(rows, axis=1).astype(np.float32)

    xs = [np.ascontiguousarray(f32("L0_attn_src")[b]) for b in range(8)]
    return c, xs, colmap, rowmap



class Sem:
    def __init__(self, h):
        self.h, self.val = h, 0

    def inc(self, inst, n=1):
        self.val += n
        inst.then_inc(self.h, n)
        return self.val


class MK:
    def __init__(self, nc, colmap, rowmap):
        self.nc, self.colmap, self.rowmap = nc, colmap, rowmap
        self.d, self.s, self.sem = {}, {}, {}
        self.pb, self.block = [], None
        self.tb_free = 0
        self.stage_gate = None
        self.hdma_last = {}

    def col(self, name, blk=0, n=1):
        off, cnt = self.colmap[name]
        assert blk + n <= cnt, name
        return self.s["colatlas"][:, off + blk: off + blk + n]

    def rowd(self, name, lo=0, n=None):
        off, cnt = self.rowmap[name]
        if n is None:
            n = cnt - lo
        return self.d["rowatlas"][:, off + lo: off + lo + n]

    def sb(self, name, shape, dtype, off):
        bpe = {F32: 4, BF16: 2}[dtype]
        n = 1
        for x in shape[1:]:
            n *= x
        assert off + n * bpe <= SB_LIMIT, f"{name}: {off + n * bpe}"
        return self.nc.alloc_sbuf_tensor_at(name, list(shape), dtype,
                                            offset=self.base + off).ap()


NTOT = 212352


def build_program(colmap, rowmap, stages=("A", "B", "C", "D")):
    nc = bass.Bass("TRN2", target_bir_lowering=False, debug=False,
                   detect_race_conditions=False)
    mk = MK(nc, colmap, rowmap)
    mk.acut = os.environ.get("MK_ACUT", "")
    arena = nc.alloc_sbuf_tensor("arena0", [128, NTOT], mybir.dt.uint8)
    mk.base = nc.lookup_mloc(arena).addr
    assert mk.base + NTOT <= 229376, mk.base
    d, s = mk.d, mk.s

    d["x"] = nc.dram_tensor("x", [NTOK, D], F32, kind="ExternalInput").ap()
    d["out"] = nc.dram_tensor("out", [NTOK, D], F32, kind="ExternalOutput").ap()
    shapes = dict(wqkv=[D, 3 * D], woutA=[D, D], w1A=[D, DFF], w2A=[DFF, D],
                  WrT=[D, 8], W1T=[8, D, DFF], W2T=[8, DFF, D],
                  qwT=[D, D], kwT=[D, D], vwT=[D, D], owT=[D, D],
                  bwT=[D, D], Acat=[D, 128], Bcat=[128, D], rwT=[D, 8],
                  ident=[128, 128])
    for nm, shp in shapes.items():
        d[nm] = nc.dram_tensor(nm, shp, F32, kind="ExternalInput").ap()
    ncols = sum(v[1] for v in colmap.values())
    nrows = sum(v[1] for v in rowmap.values())
    d["colatlas"] = nc.dram_tensor("colatlas", [128, ncols], F32, kind="ExternalInput").ap()
    d["rowatlas"] = nc.dram_tensor("rowatlas", [1, nrows], F32, kind="ExternalInput").ap()
    d["xspill"] = nc.dram_tensor("xspill", [NTOK, D], F32, kind="Internal").ap()
    d["x1spill"] = nc.dram_tensor("x1spill", [NTOK, D], F32, kind="Internal").ap()
    d["tb"] = nc.dram_tensor("tb", [NTOK, D], BF16, kind="Internal").ap()
    if os.environ.get("MK_DUMPX"):
        for st in "ABCD":
            d[f"xdbg_{st}"] = nc.dram_tensor(f"xdbg_{st}", [NTOK, D], F32, kind="Internal").ap()
    d["hsp"] = nc.dram_tensor("hsp", [32, 128, NTOK], BF16, kind="Internal").ap()

    o = 0

    def perm(name, shape, dtype):
        nonlocal o
        t = mk.sb(name, shape, dtype, o)
        bpe = {F32: 4, BF16: 2}[dtype]
        n = 1
        for x in shape[1:]:
            n *= x
        o += ((n * bpe + 31) // 32) * 32
        return t

    s["colatlas"] = perm("colatlas", [128, ncols], F32)
    s["rowbuf"] = perm("rowbuf", [1, 2112], BF16)
    s["ones_row"] = perm("ones_row", [1, 512], BF16)
    s["ones_c1"] = perm("ones_c1", [1, 128], BF16)
    s["ones_c1f"] = perm("ones_c1f", [1, 128], F32)
    s["onep64"] = perm("onep64", [128, 64], F32)
    s["stats"] = perm("stats", [128, 64], F32)
    s["stats2"] = perm("stats2", [128, 48], F32)
    s["cw"] = perm("cw", [128, 128], F32)
    s["cwD"] = perm("cwD", [128, 128], F32)
    s["kvsb"] = perm("kvsb", [128, 544], BF16)
    s["wr8"] = perm("wr8", [128, 64], BF16)
    s["ident"] = perm("ident", [128, 128], BF16)
    s["ishift"] = perm("ishift", [128, 128], BF16)
    s["g8"] = perm("g8", [128, 48], F32)
    s["const32"] = perm("const32", [128, 8], F32)
    mk.constvals = [1e-5, 1e-12, -0.5, 0.125, -1.0]
    for ci, cv in enumerate(mk.constvals):
        nc.const_aps.register(F32, cv, s["const32"][:, ci:ci + 1])
    P = ((o + 127) // 128) * 128
    mk.P = P
    assert P <= 11500, P

    mk.pb = [nc.alloc_psum_tensor(f"pb{i}", [128, 512], F32).ap() for i in range(8)]
    s["xres"] = mk.sb("xres", [128, NT * D], F32, P)

    cms = []
    for nm in ["g", "sy", "mm", "sc", "v"]:
        cm = nc.semaphore(f"sem_{nm}")
        mk.sem[nm] = Sem(cm.__enter__())
        cms.append(cm)

    blk = nc.Block()
    mk.block = blk.__enter__()

    emit_preload(mk)
    if "A" in stages:
        emit_stage_A(mk)
        emit_xdump(mk, "A")
    else:
        emit_passthrough_load(mk)
    if "B" in stages:
        emit_stage_B(mk)
        emit_xdump(mk, "B")
    if "C" in stages:
        emit_stage_C(mk)
        emit_xdump(mk, "C")
    if "D" in stages:
        emit_stage_D(mk)
        emit_xdump(mk, "D")
    emit_finish(mk)

    blk.__exit__(None, None, None)
    for cm in reversed(cms):
        cm.__exit__(None, None, None)
    return nc, mk


def emit_preload(mk):
    block, s, d = mk.block, mk.s, mk.d
    SG, SS, SV = mk.sem["g"], mk.sem["sy"], mk.sem["v"]

    @block.sync
    def _(sy):
        SS.inc(sy.dma_start(out=s["colatlas"], in_=d["colatlas"]), 16)
    mk.atlas_in = SS.val

    @block.gpsimd
    def _(g):
        SG.inc(g.dma_start(out=s["ident"], in_=d["ident"]), 16)
    ident_in = SG.val

    @block.vector
    def _(v):
        v.wait_ge(SG.h, ident_in)
        v.memset(s["ones_row"], 1.0)
        v.memset(s["ones_c1"], 1.0)
        v.memset(s["ones_c1f"], 1.0)
        v.memset(s["onep64"][64:65, :], 1.0)
        v.memset(s["kvsb"], 0.0)
        for ci, cv in enumerate(mk.constvals):
            v.memset(s["const32"][:, ci:ci + 1], cv)
        SV.inc(v.memset(s["ishift"], 0.0), 1)
        v.wait_ge(SV.h, SV.val)
        inst = v.tensor_copy(s["ishift"][0:64, 64:128], s["ident"][0:64, 0:64])
        SV.inc(inst, 1)
    mk.preload_v = SV.val


def emit_passthrough_load(mk):
    block, d = mk.block, mk.d
    SS, SV = mk.sem["sy"], mk.sem["v"]
    xsb = mk.s["xres"]

    @block.sync
    def _(sy):
        SS.inc(sy.dma_start(out=xsb.rearrange("p (t c) -> p t c", c=D),
                            in_=d["x"].rearrange("(t p) c -> p t c", p=128)), 16)
    x_in = SS.val

    @block.vector
    def _(v):
        v.wait_ge(SS.h, x_in)
        SV.inc(v.engine_nop(), 1)
    mk.stage_gate = (SV, SV.val)


def emit_xdump(mk, st):
    if not os.environ.get("MK_DUMPX"):
        return
    block, d = mk.block, mk.d
    SS = mk.sem["sy"]
    gate = mk.stage_gate

    @block.sync
    def _(sy):
        sy.wait_ge(gate[0].h, gate[1])
        SS.inc(sy.dma_start(out=d[f"xdbg_{st}"].rearrange("(t p) c -> p t c", p=128),
                            in_=mk.s["xres"].rearrange("p (t c) -> p t c", c=D)), 16)

    @block.vector
    def _(v):
        v.wait_ge(SS.h, SS.val)
        mk.sem["v"].inc(v.engine_nop(), 1)
    mk.stage_gate = (mk.sem["v"], mk.sem["v"].val)



def emit_ln_stats(mk, x_tile, junk_ap, eps, wait_sems=(), extra=None):
    block = mk.block
    S, S2 = mk.s["stats"], mk.s["stats2"]
    SC, SV = mk.sem["sc"], mk.sem["v"]

    @block.vector
    def _(v):
        for se, val in wait_sems:
            v.wait_ge(se.h, val)
        for i in range(NT):
            inst = v.reduce_sum(S[:, i:i + 1], x_tile(i), axis=AX.X)
        SV.inc(inst, 1)
    v_sum = SV.val

    @block.scalar
    def _(sc):
        for se, val in wait_sems:
            sc.wait_ge(se.h, val)
        for i in range(NT):
            inst = sc.activation(junk_ap, x_tile(i), AF.Square, accum_out=S[:, 16 + i:17 + i])
        SC.inc(inst, 1)
    c_ssq = SC.val

    if mk.nc and getattr(mk, "acut", "") == "stats0":
        @block.vector
        def _(v):
            v.wait_ge(SC.h, c_ssq)
            v.wait_ge(SV.h, v_sum)
            inst = v.tensor_copy(mk.s["xres"][:, 0:64], S)
            SV.inc(inst, 1)
        mk.stage_gate = (SV, SV.val)
        mk._cut_hit = True
        return SV.val

    @block.vector
    def _(v):
        v.wait_ge(SC.h, c_ssq)
        v.wait_ge(SV.h, v_sum)
        SV.inc(v.tensor_scalar_mul(S[:, 32:48], S[:, 0:16], 1.0 / D), 1)
        v.wait_ge(SV.h, SV.val)
        SV.inc(v.tensor_mul(S[:, 48:64], S[:, 32:48], S[:, 32:48]), 1)
        v.wait_ge(SV.h, SV.val)
        inst = v.scalar_tensor_tensor(S2[:, 32:48], S[:, 16:32], 1.0 / D, S[:, 48:64],
                                      ALU.mult, ALU.subtract)
        SV.inc(inst, 1)

    @block.scalar
    def _(sc):
        sc.wait_ge(SV.h, SV.val)
        SC.inc(sc.activation(S[:, 48:64], S2[:, 32:48], AF.Ln, bias=float(eps)), 1)
        sc.wait_ge(SC.h, SC.val)
        inst = sc.activation(S2[:, 0:16], S[:, 48:64], AF.Exp, scale=-0.5)
        SC.inc(inst, 1)

    @block.vector
    def _(v):
        v.wait_ge(SC.h, SC.val)
        inst = v.scalar_tensor_tensor(S2[:, 16:32], S[:, 32:48], -1.0, S2[:, 0:16],
                                      ALU.mult, ALU.mult)
        if extra is not None:
            def sync():
                SV.inc(v.engine_nop(), 0) if False else None
            inst = extra(v, SV)
        SV.inc(inst, 1)
    return SV.val


def emit_ln_apply_bounce(mk, x_tile, stg32, dstT, stats_ev):
    block, d = mk.block, mk.d
    SG, SS, SC = mk.sem["g"], mk.sem["sy"], mk.sem["sc"]
    SV, S2 = mk.sem["v"], mk.s["stats2"]
    C0 = SC.val
    g0 = SG.val
    e_dma = [g0 + 16 * (i + 1) for i in range(NT)]
    tbf = mk.tb_free

    @block.scalar
    def _(sc):
        sc.wait_ge(SV.h, stats_ev)
        for i in range(NT):
            if i >= 4:
                sc.wait_ge(SG.h, e_dma[i - 4])
            inst = sc.activation(stg32[:, (i % 4) * 1024:(i % 4) * 1024 + 1024],
                                 x_tile(i), AF.Identity,
                                 bias=S2[:, 16 + i:17 + i], scale=S2[:, i:i + 1])
            SC.inc(inst, 1)

    @block.gpsimd
    def _(g):
        if tbf:
            g.wait_ge(SS.h, tbf)
        for i in range(NT):
            g.wait_ge(SC.h, C0 + i + 1)
            SG.inc(g.dma_start(out=d["tb"][i * 128:(i + 1) * 128, :],
                               in_=stg32[:, (i % 4) * 1024:(i % 4) * 1024 + 1024]), 16)
    sg_done = SG.val

    @block.sync
    def _(sy):
        sy.wait_ge(SG.h, sg_done)
        for cb in range(CBN):
            SS.inc(sy.dma_start(out=dstT(cb), in_=d["tb"][:, cb * 128:(cb + 1) * 128],
                                transpose=True), 16)
    mk.tb_free = SS.val
    return SS.val, sg_done


def bounce_transpose(mk, src3, dst_cb, wait=None, ncols=D):
    block, d = mk.block, mk.d
    SS = mk.sem["sy"]
    nblk = ncols // 128

    @block.sync
    def _(sy):
        if wait is not None:
            sy.wait_ge(wait[0].h, wait[1])
        if mk.tb_free:
            sy.wait_ge(SS.h, mk.tb_free)
        SS.inc(sy.dma_start(out=d["tb"].rearrange("(t p) c -> p t c", p=128)[:, :, 0:ncols],
                            in_=src3), 16)
        wr = SS.val
        sy.wait_ge(SS.h, wr)
        for cb in range(nblk):
            SS.inc(sy.dma_start(out=dst_cb(cb), in_=d["tb"][:, cb * 128:(cb + 1) * 128],
                                transpose=True), 16)
    mk.tb_free = SS.val
    return SS.val



def emit_expert(mk, w1_dram3, w2_dram3, b1col, b2row_sb, rhsT, w1dbl, hstage,
                w2t, hstream, evac_cb, w1_gate=(), w2_gate=(), rhs_gate=(),
                evac_wait=None, gelu_gate=()):
    nc, block, d, s, pb = mk.nc, mk.block, mk.d, mk.s, mk.pb
    SG, SS, SM, SC, SV = (mk.sem[k] for k in ["g", "sy", "mm", "sc", "v"])
    FC, FB, NCH = 8, 4, 4

    g1_base = SG.val
    mm_base = SM.val
    e_mm = {}
    jj = 0
    for fc in range(FC):
        for ch in range(NCH):
            for fb in range(FB):
                e_mm[(fc, ch, fb)] = mm_base + jj + 1
                jj += 1
    ph1_last = mm_base + FC * FB * NCH
    sc_base = SC.val
    e_gelu = {}
    jj = 0
    for fc in range(FC):
        for ch in range(NCH):
            for fb in range(FB):
                e_gelu[(fc, ch, fb)] = sc_base + jj + 1
                jj += 1

    @block.gpsimd
    def _(g):
        for se, val in w1_gate:
            g.wait_ge(se.h, val)
        for fc in range(FC):
            if fc >= 2:
                g.wait_ge(SM.h, e_mm[(fc - 2, NCH - 1, FB - 1)])
            SG.inc(g.dma_start(out=w1dbl[:, (fc % 2) * 4096:(fc % 2) * 4096 + 4096]
                               .rearrange("p (cb f) -> p cb f", f=512),
                               in_=w1_dram3[:, :, fc * 512:(fc + 1) * 512]), 16)
    e_w1 = [g1_base + 16 * (fc + 1) for fc in range(FC)]

    ss_h_base = SS.val
    e_hdma = {}
    k = 0
    for fc in range(FC):
        for ch in range(NCH):
            e_hdma[(fc, ch)] = ss_h_base + 16 * (k + 1)
            k += 1
    prev_hdma = dict(mk.hdma_last)

    def hdma_before(fc, ch):
        g = fc * NCH + ch - 2
        if g < 0:
            return prev_hdma.get(ch % 2)
        return e_hdma[(g // NCH, g % NCH)]

    @block.tensor
    def _(t):
        for se, val in rhs_gate:
            t.wait_ge(se.h, val)
        jj = 0
        for fc in range(FC):
            t.wait_ge(SG.h, e_w1[fc])
            for ch in range(NCH):
                for fb in range(FB):
                    if jj >= 4:
                        t.wait_ge(SC.h, sc_base + jj - 3)
                    ps = pb[jj % 4]
                    for cb in range(CBN):
                        inst = t.matmul(ps, w1dbl[:, (fc % 2) * 4096 + cb * 512 + fb * 128:
                                        (fc % 2) * 4096 + cb * 512 + (fb + 1) * 128],
                                        rhsT[:, cb * NTOK + ch * 512: cb * NTOK + (ch + 1) * 512],
                                        start=(cb == 0), stop=(cb == CBN - 1))
                    SM.inc(inst, 1)
                    jj += 1

    @block.scalar
    def _(sc):
        for se, val in gelu_gate:
            sc.wait_ge(se.h, val)
        jj = 0
        for fc in range(FC):
            for ch in range(NCH):
                slot = (fc * NCH + ch) % 2
                for fb in range(FB):
                    sc.wait_ge(SM.h, e_mm[(fc, ch, fb)])
                    if fb == 0:
                        prev = hdma_before(fc, ch)
                        if prev is not None:
                            sc.wait_ge(SS.h, prev)
                    inst = sc.activation(
                        hstage[:, slot * 2048 + fb * 512: slot * 2048 + (fb + 1) * 512],
                        pb[jj % 4], AF.Gelu, bias=mk.col(b1col, fc * FB + fb))
                    SC.inc(inst, 1)
                    jj += 1

    @block.sync
    def _(sy):
        for fc in range(FC):
            for ch in range(NCH):
                slot = (fc * NCH + ch) % 2
                sy.wait_ge(SC.h, e_gelu[(fc, ch, FB - 1)])
                SS.inc(sy.dma_start(
                    out=d["hsp"][fc * 4:(fc + 1) * 4, :, ch * 512:(ch + 1) * 512]
                    .rearrange("fb p c -> p fb c"),
                    in_=hstage[:, slot * 2048:(slot + 1) * 2048]
                    .rearrange("p (fb c) -> p fb c", c=512)), 16)
    h_written = SS.val
    mk.hdma_last[0] = e_hdma[(FC - 1, NCH - 2)]
    mk.hdma_last[1] = e_hdma[(FC - 1, NCH - 1)]

    mm2_base = SM.val
    e_mm2 = [mm2_base + 4 * (w + 1) for w in range(8)]
    v2_base = SV.val
    e_ev2 = [v2_base + 4 * (w + 1) for w in range(8)]
    mk.e_ev2 = e_ev2
    ss2_base = SS.val
    e_hstr = [ss2_base + 16 * (w + 1) for w in range(8)]

    @block.sync
    def _(sy):
        for w in range(8):
            if w == 0:
                sy.wait_ge(SM.h, ph1_last)
                sy.wait_ge(SS.h, e_hdma[(FC - 1, 0)])
            elif w == 1:
                sy.wait_ge(SS.h, h_written)
            else:
                sy.wait_ge(SS.h, e_hdma[(FC - 1, w // 2)])
                sy.wait_ge(SM.h, e_mm2[w - 2])
            SS.inc(sy.dma_start(out=hstream[:, (w % 2) * 8192:(w % 2) * 8192 + 8192]
                                .rearrange("p (fb c) -> p fb c", c=256),
                                in_=d["hsp"][:, :, w * 256:(w + 1) * 256]
                                .rearrange("fb p c -> p fb c")), 16)

    @block.tensor
    def _(t):
        for se, val in w2_gate:
            t.wait_ge(se.h, val)
        t.wait_ge(SV.h, v2_base)
        for w in range(8):
            t.wait_ge(SS.h, e_hstr[w])
            if w >= 1:
                t.wait_ge(SV.h, e_ev2[w - 1])
            for ntl in range(2):
                for chk in range(2):
                    ps = pb[4 + ntl * 2 + chk]
                    for fb in range(32):
                        t.matmul(ps, hstream[:, (w % 2) * 8192 + fb * 256 + ntl * 128:
                                             (w % 2) * 8192 + fb * 256 + ntl * 128 + 128],
                                 w2t[:, fb * 1024 + chk * 512: fb * 1024 + (chk + 1) * 512],
                                 start=(fb == 0), stop=False)
                    inst = t.matmul(ps, s["ones_c1"], b2row_sb[:, chk * 512:(chk + 1) * 512],
                                    start=False, stop=True)
                    SM.inc(inst, 1)

    @block.vector
    def _(v):
        for w in range(8):
            if evac_wait is not None:
                for se, val in evac_wait(w):
                    v.wait_ge(se.h, val)
            for ntl in range(2):
                for chk in range(2):
                    v.wait_ge(SM.h, mm2_base + w * 4 + ntl * 2 + chk + 1)
                    inst = evac_cb(v, w, ntl, pb[4 + ntl * 2 + chk], chk)
                    SV.inc(inst, 1)
    assert SM.val == mm2_base + 32
    assert mm2_base == ph1_last
    return SV.val, ph1_last, (e_mm2[-1], e_mm2[-2])


def emit_dump_bf16(mk, src_full, ncols, gate):
    block = mk.block
    SV = mk.sem["v"]

    @block.vector
    def _(v):
        v.wait_ge(gate[0].h, gate[1])
        for j in range(0, ncols, 1024):
            w = min(1024, ncols - j)
            inst = v.tensor_copy(mk.s["xres"][:, j:j + w], src_full[:, j:j + w])
        SV.inc(inst, 1)
    mk.stage_gate = (SV, SV.val)



def emit_stage_A(mk):
    nc, block, d, s, pb = mk.nc, mk.block, mk.d, mk.s, mk.pb
    SG, SS, SM, SC, SV = (mk.sem[k] for k in ["g", "sy", "mm", "sc", "v"])
    P = mk.P
    S, S2 = s["stats"], s["stats2"]

    xsb = s["xres"]
    ln1 = mk.sb("ln1_A", [128, NT * D], BF16, P + 65536)
    R2 = mk.sb("R2_A", [128, NT * D], BF16, P + 98304)
    wsl = mk.sb("wsl_A", [128, NT * D], BF16, P + 131072)
    qfT = mk.sb("qfT_A", [128, NT * D], BF16, P + 163840)
    junk1 = mk.sb("junk1_A", [128, 2 * 1024], F32, P + 98304)
    stg32A = mk.sb("stg32_A", [128, 4 * 1024], F32, P + 106496)
    scr = mk.sb("scr_A", [128, 2048], F32, P)
    scr2 = mk.sb("scr2_A", [128, 2048], F32, P + 8192)
    scre = mk.sb("scre_A", [128, 2048], F32, P + 16384)
    kfv = mk.sb("kfv_A", [128, 2 * 1024 + 2 * 1040], BF16, P + 24576)
    junk2 = mk.sb("junk2_A", [128, 1024], F32, P + 98304)
    kvacc = mk.sb("kvacc_A", [128, 544], F32, P + 33024)
    s["kvacc"] = kvacc
    w1dbl = mk.sb("w1dbl_A", [128, 2 * 4096], BF16, P)
    hstage = mk.sb("hstage_A", [128, 2 * 2048], BF16, P + 16384)
    w2t = mk.sb("w2t_A", [128, 32 * 1024], BF16, P + 98304)
    hstream = mk.sb("hstream_A", [128, 2 * 8192], BF16, P + 65536)
    xx1 = [mk.sb(f"xx1_{k}_A", [128, 2 * 1024], F32, P + 163840 + k * 8192) for k in range(2)]

    def xt(i):
        return xsb[:, i * D:(i + 1) * D]

    def ln1t(i):
        return ln1[:, i * D:(i + 1) * D]

    @block.sync
    def _(sy):
        SS.inc(sy.dma_start(out=xsb.rearrange("p (t c) -> p t c", c=D),
                            in_=d["x"].rearrange("(t p) c -> p t c", p=128)), 16)
    x_in = SS.val
    if mk.acut == "xload":
        mk.stage_gate = (SS, x_in)
        return

    def r2_fold(v, SVl):
        SVl.inc(v.tensor_mul(S[:, 0:16], S2[:, 0:16], S2[:, 0:16]), 1)
        v.wait_ge(SVl.h, SVl.val)
        return v.tensor_mul(S[:, 0:16], S[:, 0:16], S2[:, 32:48])

    emit_ln_stats(mk, xt, junk1[:, 0:1024], LN_EPS,
                  wait_sems=[(SS, x_in), (SS, mk.atlas_in)], extra=r2_fold)
    if getattr(mk, "_cut_hit", False):
        return

    @block.scalar
    def _(sc):
        sc.wait_ge(SV.h, SV.val)
        SC.inc(sc.activation(S[:, 48:64], S[:, 0:16], AF.Ln, bias=float(LN_EPS)), 1)
        sc.wait_ge(SC.h, SC.val)
        inst = sc.activation(S[:, 16:32], S[:, 48:64], AF.Exp, scale=-0.5)
        SC.inc(inst, 1)

    @block.vector
    def _(v):
        v.wait_ge(SC.h, SC.val)
        v.tensor_mul(S[:, 32:48], S2[:, 0:16], S[:, 16:32])
        inst = v.tensor_mul(S[:, 48:64], S2[:, 16:32], S[:, 16:32])
        SV.inc(inst, 1)
    comb = SV.val
    if mk.acut == "stats":
        @block.vector
        def _(v):
            v.wait_ge(SV.h, comb)
            v.tensor_copy(s["xres"][:, 0:64], S)
            inst = v.tensor_copy(s["xres"][:, 64:112], S2)
            SV.inc(inst, 1)
        mk.stage_gate = (SV, SV.val)
        return

    @block.scalar
    def _(sc):
        sc.wait_ge(SV.h, comb)
        for i in range(NT):
            sc.activation(ln1t(i), xt(i), AF.Identity,
                          bias=S2[:, 16 + i:17 + i], scale=S2[:, i:i + 1])
            inst = sc.activation(R2[:, i * D:(i + 1) * D], xt(i), AF.Identity,
                                 bias=S[:, 48 + i:49 + i], scale=S[:, 32 + i:33 + i])
        SC.inc(inst, 1)
    t_done = SC.val

    @block.sync
    def _(sy):
        sy.wait_ge(SC.h, t_done)
        SS.inc(sy.dma_start(out=d["xspill"].rearrange("(t p) c -> p t c", p=128),
                            in_=xsb.rearrange("p (t c) -> p t c", c=D)), 16)
    x_spilled = SS.val

    if mk.acut == "ln1a":
        emit_dump_bf16(mk, R2, NT * D, (SC, t_done))
        return
    if mk.acut == "ln1z":
        emit_dump_bf16(mk, ln1, NT * D, (SC, t_done))
        return
    aT_in = bounce_transpose(mk, R2.rearrange("p (t c) -> p t c", c=D),
                             lambda cb: R2[:, cb * NTOK:(cb + 1) * NTOK],
                             wait=(SC, t_done))
    if mk.acut == "ln1b":
        @block.vector
        def _(v):
            v.wait_ge(SS.h, aT_in)
            SV.inc(v.engine_nop(), 1)
        emit_dump_bf16(mk, R2, NT * D, (SV, SV.val))
        return

    @block.gpsimd
    def _(g):
        g.wait_ge(SS.h, x_spilled)
        SG.inc(g.dma_start(out=wsl.rearrange("p (cb k) -> p cb k", k=2048),
                           in_=d["wqkv"].rearrange("(cb p) k -> p cb k", p=128)[:, :, 1024:3072]), 16)
        SG.inc(g.dma_start(out=s["rowbuf"][:, 0:2048], in_=mk.rowd("kvbA")), 16)
    wkv_in = SG.val

    @block.vector
    def _(v):
        v.wait_ge(SS.h, aT_in)
        for cb in range(CBN):
            a = R2[:, cb * NTOK:(cb + 1) * NTOK]
            inst = v.tensor_scalar(a, a, mk.col("g1", cb), mk.col("be1", cb), ALU.mult, ALU.add)
        SV.inc(inst, 1)
    aT_rdy = SV.val
    if mk.acut == "ln1":
        emit_dump_bf16(mk, R2, NT * D, (SV, aT_rdy))
        return

    @block.vector
    def _(v):
        v.wait_ge(SS.h, x_spilled)
        v.memset(s["kvacc"], 0.0)
        v.memset(pb[6], 0.0)
        inst = v.memset(pb[7], 0.0)
        for r in range(2):
            off = 2048 + r * 1040
            for h in range(NH):
                inst = v.memset(kfv[:, off + h * 65 + 64: off + h * 65 + 65], 1.0)
        SV.inc(inst, 1)
    ones_rdy = SV.val

    def kf_tile(i):
        return kfv[:, (i % 2) * 1024:(i % 2) * 1024 + 1024]

    def v_tile(i):
        off = 2048 + (i % 2) * 1040
        return kfv[:, off:off + 1040]

    def kv_ps(h):
        grp = h % 2
        return pb[6 + h // 8][64 * grp:64 * grp + 64,
                              ((h % 8) // 2) * 68:((h % 8) // 2) * 68 + 65], grp

    M0, V0, C0 = SM.val, SV.val, SC.val

    @block.tensor
    def _(t):
        t.wait_ge(SV.h, aT_rdy)
        t.wait_ge(SG.h, wkv_in)
        for i in range(NT):
            if i >= 1:
                t.wait_ge(SV.h, V0 + 3 * (i - 1) + 1)
            for ch in range(4):
                ps = pb[ch]
                for cb in range(CBN):
                    t.matmul(ps, R2[:, cb * NTOK + i * 128: cb * NTOK + (i + 1) * 128],
                             wsl[:, cb * 2048 + ch * 512: cb * 2048 + (ch + 1) * 512],
                             start=(cb == 0), stop=False)
                inst = t.matmul(ps, s["ones_c1"], s["rowbuf"][:, ch * 512:(ch + 1) * 512],
                                start=False, stop=True)
            SM.inc(inst, 1)
            t.wait_ge(SV.h, V0 + 3 * i + 2)
            for h in range(0, NH, 2):
                pskv, grp = kv_ps(h)
                inst = t.matmul(pskv, kf_tile(i)[:, h * 64:(h + 1) * 64],
                                v_tile(i)[:, h * 65:(h + 1) * 65],
                                start=True, stop=True, tile_position=(0, 64 * grp))
            SM.inc(inst, 1)
            t.wait_ge(SM.h, SM.val)
            for h in range(1, NH, 2):
                pskv, grp = kv_ps(h)
                inst = t.matmul(pskv, kf_tile(i)[:, h * 64:(h + 1) * 64],
                                v_tile(i)[:, h * 65:(h + 1) * 65],
                                start=True, stop=True, tile_position=(0, 64 * grp))
            SM.inc(inst, 1)

    @block.vector
    def _(v):
        v.wait_ge(SV.h, ones_rdy)
        for i in range(NT):
            v.wait_ge(SM.h, M0 + 3 * i + 1)
            if i >= 2:
                v.wait_ge(SC.h, C0 + i - 1)
            v.tensor_scalar(scr[:, 0:512], pb[0], 0.0, None, ALU.min)
            v.tensor_scalar(scr[:, 512:1024], pb[1], 0.0, None, ALU.min)
            v.tensor_scalar(scr2[:, 0:512], pb[0], 0.0, None, ALU.max)
            v.tensor_scalar(scr2[:, 512:1024], pb[1], 0.0, None, ALU.max)
            v.tensor_copy(v_tile(i).rearrange("p (h c) -> p h c", c=65)[:, 0:8, 0:64],
                          pb[2].rearrange("p (h c) -> p h c", c=64))
            inst = v.tensor_copy(v_tile(i).rearrange("p (h c) -> p h c", c=65)[:, 8:16, 0:64],
                                 pb[3].rearrange("p (h c) -> p h c", c=64))
            SV.inc(inst, 1)
            v.wait_ge(SC.h, C0 + i + 1)
            inst = v.tensor_add(kf_tile(i), scre[:, 0:1024], scr2[:, 0:1024])
            SV.inc(inst, 1)
            v.wait_ge(SM.h, M0 + 3 * i + 3)
            v.tensor_add(s["kvacc"][:, 0:272], s["kvacc"][:, 0:272], pb[6][:, 0:272])
            inst = v.tensor_add(s["kvacc"][:, 272:544], s["kvacc"][:, 272:544], pb[7][:, 0:272])
            SV.inc(inst, 1)

    @block.scalar
    def _(sc):
        for i in range(NT):
            sc.wait_ge(SV.h, V0 + 3 * i + 1)
            inst = sc.activation(scre[:, 0:1024], scr[:, 0:1024], AF.Exp)
            SC.inc(inst, 1)
    pass1_v = V0 + 3 * NT

    @block.vector
    def _(v):
        v.wait_ge(SV.h, pass1_v)
        inst = v.tensor_copy(s["kvsb"], s["kvacc"])
        SV.inc(inst, 1)
    kvsb_rdy = SV.val
    if mk.acut == "pass1":
        emit_dump_bf16(mk, s["kvsb"], 544, (SV, kvsb_rdy))
        return

    @block.gpsimd
    def _(g):
        g.wait_ge(SM.h, M0 + 3 * (NT - 1) + 1)
        SG.inc(g.dma_start(out=wsl[:, 0:8192].rearrange("p (cb k) -> p cb k", k=1024),
                           in_=d["wqkv"].rearrange("(cb p) k -> p cb k", p=128)[:, :, 0:1024]), 16)
        SG.inc(g.dma_start(out=wsl[:, 8192:16384].rearrange("p (cb k) -> p cb k", k=1024),
                           in_=d["woutA"].rearrange("(cb p) k -> p cb k", p=128)), 16)
        SG.inc(g.dma_start(out=s["rowbuf"][:, 0:1024], in_=mk.rowd("outbA")), 16)
    wq_in = SG.val - 32
    wout_in = SG.val
    if mk.acut == "p2load":
        mk.stage_gate = (SG, wout_in)
        return

    M1, V1, C1 = SM.val, SV.val, SC.val
    NP2 = 6 if mk.acut in ("p2mm6", "p2ev6", "p2v6", "p2e6") else 32

    @block.tensor
    def _(t):
        t.wait_ge(SG.h, wq_in)
        t.wait_ge(SV.h, pass1_v)
        for j in range(NP2):
            ob, ch = divmod(j, 4)
            if j >= 6:
                t.wait_ge(SV.h, V1 + 2 * (j - 6) + 2)
            ps = pb[j % 6]
            for cb in range(CBN):
                inst = t.matmul(ps, wsl[:, cb * 1024 + ob * 128: cb * 1024 + (ob + 1) * 128],
                                R2[:, cb * NTOK + ch * 512: cb * NTOK + (ch + 1) * 512],
                                start=(cb == 0), stop=(cb == CBN - 1))
            SM.inc(inst, 1)
    if mk.acut == "p2mm6":
        mk.stage_gate = (SM, M1 + 6)
        return
    if mk.acut in ("p2v6", "p2e6"):
        @block.vector
        def _(v):
            for j in range(NP2):
                ob, ch = divmod(j, 4)
                v.wait_ge(SM.h, M1 + j + 1)
                sl = (j % 2) * 512
                v.tensor_scalar(scr[:, sl:sl + 512], pb[j % 6], mk.col("qbA", ob), 0.0,
                                ALU.add, ALU.min)
                inst = v.tensor_scalar(scr2[:, sl:sl + 512], pb[j % 6], mk.col("qbA", ob), 0.0,
                                       ALU.add, ALU.max)
                SV.inc(inst, 1)
        if mk.acut == "p2v6":
            mk.stage_gate = (SV, V1 + 6)
            return
        @block.scalar
        def _(sc):
            for j in range(NP2):
                sc.wait_ge(SV.h, V1 + j + 1)
                sl = (j % 2) * 512
                inst = sc.activation(scre[:, sl:sl + 512], scr[:, sl:sl + 512], AF.Exp)
                SC.inc(inst, 1)
        mk.stage_gate = (SC, C1 + 6)
        return

    @block.vector
    def _(v):
        for j in range(NP2):
            ob, ch = divmod(j, 4)
            v.wait_ge(SM.h, M1 + j + 1)
            if j >= 2:
                v.wait_ge(SC.h, C1 + j - 1)
            sl = (j % 2) * 512
            v.tensor_scalar(scr[:, sl:sl + 512], pb[j % 6], mk.col("qbA", ob), 0.0,
                            ALU.add, ALU.min)
            inst = v.tensor_scalar(scr2[:, sl:sl + 512], pb[j % 6], mk.col("qbA", ob), 0.0,
                                   ALU.add, ALU.max)
            SV.inc(inst, 1)
            v.wait_ge(SC.h, C1 + j + 1)
            inst = v.tensor_add(qfT[:, ob * NTOK + ch * 512: ob * NTOK + (ch + 1) * 512],
                                scre[:, sl:sl + 512], scr2[:, sl:sl + 512])
            SV.inc(inst, 1)

    @block.scalar
    def _(sc):
        for j in range(NP2):
            sc.wait_ge(SV.h, V1 + 2 * j + 1)
            sl = (j % 2) * 512
            inst = sc.activation(scre[:, sl:sl + 512], scr[:, sl:sl + 512], AF.Exp)
            SC.inc(inst, 1)
    pass2_v = V1 + 2 * NP2
    if mk.acut == "p2ev6":
        mk.stage_gate = (SV, pass2_v)
        return
    if mk.acut == "pass2":
        emit_dump_bf16(mk, qfT, NT * D, (SV, pass2_v))
        return

    M2, V2 = SM.val, SV.val
    NT3 = 2 if mk.acut in ("p3mm2", "p3ev2") else NT

    def y_ps(i, h):
        return pb[(i % 2) * 3 + h // 7][:, (h % 7) * 72:(h % 7) * 72 + 65]

    @block.tensor
    def _(t):
        t.wait_ge(SV.h, kvsb_rdy)
        t.wait_ge(SV.h, pass2_v)
        for i in range(NT3):
            if i >= 2:
                t.wait_ge(SV.h, V2 + 4 * (i - 1))
            for par in range(2):
                for h in range(par, NH, 2):
                    grp = h % 2
                    qs = qfT[64 * grp:64 * grp + 64,
                             (h // 2) * NTOK + i * 128:(h // 2) * NTOK + (i + 1) * 128]
                    ks = s["kvsb"][64 * grp:64 * grp + 64,
                                   (h // 8) * 272 + ((h % 8) // 2) * 68:
                                   (h // 8) * 272 + ((h % 8) // 2) * 68 + 65]
                    inst = t.matmul(y_ps(i, h), qs, ks, start=True, stop=True,
                                    tile_position=(64 * grp, 0))
                SM.inc(inst, 1)
                if par == 0:
                    t.wait_ge(SM.h, SM.val)
    if mk.acut == "p3mm2":
        mk.stage_gate = (SM, M2 + 4)
        return

    @block.vector
    def _(v):
        for i in range(NT3):
            v.wait_ge(SM.h, M2 + 2 * i + 2)
            b0 = (i % 2) * 3
            v.tensor_copy(S[:, 48:55],
                          pb[b0][:, 0:504].rearrange("p (s c) -> p s c", c=72)[:, :, 64:65].squeeze(2))
            v.tensor_copy(S[:, 55:62],
                          pb[b0 + 1][:, 0:504].rearrange("p (s c) -> p s c", c=72)[:, :, 64:65].squeeze(2))
            SV.inc(v.tensor_copy(S[:, 62:64],
                   pb[b0 + 2][:, 0:144].rearrange("p (s c) -> p s c", c=72)[:, :, 64:65].squeeze(2)), 1)
            v.wait_ge(SV.h, SV.val)
            SV.inc(v.tensor_scalar(S[:, 32:48], S[:, 48:64], 1e-6, None, ALU.max), 1)
            v.wait_ge(SV.h, SV.val)
            SV.inc(v.reciprocal(S[:, 48:64], S[:, 32:48]), 1)
            v.wait_ge(SV.h, SV.val)
            for h in range(NH):
                inst = v.tensor_scalar(R2[:, i * D + h * 64: i * D + (h + 1) * 64],
                                       y_ps(i, h)[:, 0:64], S[:, 48 + h:49 + h], None, ALU.mult)
            SV.inc(inst, 1)
    y_done = SV.val
    if mk.acut == "p3ev2":
        mk.stage_gate = (SV, V2 + 8)
        return

    if mk.acut == "y":
        emit_dump_bf16(mk, R2, NT * D, (SV, y_done))
        return
    yT_in = bounce_transpose(mk, R2.rearrange("p (t c) -> p t c", c=D),
                             lambda cb: R2[:, cb * NTOK:(cb + 1) * NTOK],
                             wait=(SV, y_done))

    M3, V3 = SM.val, SV.val

    @block.tensor
    def _(t):
        t.wait_ge(SS.h, yT_in)
        t.wait_ge(SG.h, wout_in)
        for i in range(NT):
            if i >= 3:
                t.wait_ge(SV.h, V3 + (i - 3) + 1)
            for chk in range(2):
                ps = pb[(i % 3) * 2 + chk]
                for cb in range(CBN):
                    t.matmul(ps, R2[:, cb * NTOK + i * 128: cb * NTOK + (i + 1) * 128],
                             wsl[:, 8192 + cb * 1024 + chk * 512:
                                 8192 + cb * 1024 + (chk + 1) * 512],
                             start=(cb == 0), stop=False)
                inst = t.matmul(ps, s["ones_c1"],
                                s["rowbuf"][:, chk * 512:(chk + 1) * 512],
                                start=False, stop=True)
            SM.inc(inst, 1)

    @block.vector
    def _(v):
        for i in range(NT):
            v.wait_ge(SM.h, M3 + i + 1)
            for chk in range(2):
                inst = v.scalar_tensor_tensor(
                    xt(i)[:, chk * 512:(chk + 1) * 512], pb[(i % 3) * 2 + chk], 1.0,
                    ln1t(i)[:, chk * 512:(chk + 1) * 512], ALU.mult, ALU.add)
            SV.inc(inst, 1)
    x1_done = SV.val

    @block.sync
    def _(sy):
        sy.wait_ge(SV.h, x1_done)
        SS.inc(sy.dma_start(out=d["x1spill"].rearrange("(t p) c -> p t c", p=128),
                            in_=xsb.rearrange("p (t c) -> p t c", c=D)), 16)
    x1_spilled = SS.val
    if mk.acut == "pass5":
        mk.stage_gate = (SS, x1_spilled)
        return

    ev = emit_ln_stats(mk, xt, junk2[:, 0:1024], LN_EPS, wait_sems=[(SV, x1_done)])

    a2T_in, t2_dma_done = emit_ln_apply_bounce(
        mk, xt, stg32A, lambda cb: R2[:, cb * NTOK:(cb + 1) * NTOK], ev)
    t2_done = t2_dma_done

    @block.vector
    def _(v):
        v.wait_ge(SS.h, a2T_in)
        for cb in range(CBN):
            a = R2[:, cb * NTOK:(cb + 1) * NTOK]
            inst = v.tensor_scalar(a, a, mk.col("g2", cb), mk.col("be2", cb), ALU.mult, ALU.add)
        SV.inc(inst, 1)
    a2T_rdy = SV.val
    if mk.acut == "ln2":
        mk.stage_gate = (SV, a2T_rdy)
        return

    ph1_last_pred = SM.val + 128

    @block.gpsimd
    def _(g):
        SG.inc(g.dma_start(out=s["rowbuf"][:, 1024:2048], in_=mk.rowd("b2A")), 16)

    w2A_in = SG.val + 128 + 16
    lg_base = w2A_in
    e_load = [lg_base + 64 * (w + 1) for w in range(8)]

    def evac_A(v, w, ntl, ps, chk):
        i = 2 * w + ntl
        dst = xt(i)[:, chk * 512:(chk + 1) * 512]
        xb = xx1[ntl][:, (w % 2) * 1024 + chk * 512:(w % 2) * 1024 + (chk + 1) * 512]
        return v.scalar_tensor_tensor(dst, ps, 1.0, xb, ALU.mult, ALU.add)

    ev_ffn, ph1_last, (ph2_last, _p26) = emit_expert(
        mk, d["w1A"].rearrange("(cb p) f -> p cb f", p=128),
        d["w2A"].rearrange("(fb p) c -> p fb c", p=128),
        "b1A", s["rowbuf"][:, 1024:2048], R2, w1dbl, hstage, w2t, hstream,
        evac_A,
        w1_gate=[(SG, t2_dma_done)], w2_gate=[(SG, w2A_in)], rhs_gate=[(SV, a2T_rdy)],
        evac_wait=lambda w: [(SG, e_load[w])],
    )
    assert ph1_last == ph1_last_pred, (ph1_last, ph1_last_pred)

    @block.gpsimd
    def _(g):
        g.wait_ge(SM.h, ph1_last_pred)
        SG.inc(g.dma_start(out=w2t.rearrange("p (fb c) -> p fb c", c=1024),
                           in_=d["w2A"].rearrange("(fb p) c -> p fb c", p=128)), 16)
        assert SG.val == w2A_in, (SG.val, w2A_in)

    @block.gpsimd
    def _(g):
        assert SG.val == lg_base, (SG.val, lg_base)
        g.wait_ge(SS.h, x1_spilled)
        for w in range(8):
            if w >= 2:
                g.wait_ge(SV.h, mk.e_ev2[w - 2])
            for ntl in range(2):
                i = 2 * w + ntl
                buf = xx1[ntl][:, (w % 2) * 1024:(w % 2) * 1024 + 1024]
                SG.inc(g.dma_start(out=buf, in_=d["x1spill"][i * 128:(i + 1) * 128, :]), 16)
                g.wait_ge(SG.h, SG.val)
                SG.inc(g.dma_start(out=buf, in_=d["xspill"][i * 128:(i + 1) * 128, :],
                                   accum_op=ALU.add), 16)

    mk.stage_gate = (SV, ev_ffn)


def emit_gates(mk, Mr_base, cw_ap, scratch, scale2=None, pb_pair=(0, 1)):
    block, s, pb = mk.block, mk.s, mk.pb
    SC, SV = mk.sem["sc"], mk.sem["v"]
    G8 = s["g8"]
    u = scratch[:, 0:128]
    msk = scratch[:, 128:256]
    w = scratch[:, 256:384]
    expu = scratch[:, 384:512]
    V0 = SV.val
    e_copy = [V0 + i + 1 for i in range(NT)]

    @block.vector
    def _(v):
        for i in range(NT):
            v.wait_ge(mk.sem["mm"].h, Mr_base + i + 1)
            SV.inc(v.tensor_copy(cw_ap[:, i * 8:(i + 1) * 8], pb[pb_pair[i % 2]][:, 0:8]), 1)
        v.wait_ge(SV.h, SV.val)
        SV.inc(v.reduce_max(G8[:, 0:16], cw_ap.rearrange("p (t e) -> p t e", e=8), axis=AX.X), 1)
        v.wait_ge(SV.h, SV.val)
        for i in range(NT):
            inst = v.tensor_scalar(u[:, i * 8:(i + 1) * 8], cw_ap[:, i * 8:(i + 1) * 8],
                                   G8[:, i:i + 1], None, ALU.subtract)
        SV.inc(inst, 1)
        u_rdy = SV.val
        v.wait_ge(SV.h, SV.val)
        SV.inc(v.tensor_scalar(msk[:], u[:], 0.0, None, ALU.is_equal), 1)
        v.wait_ge(SV.h, SV.val)
        SV.inc(v.scalar_tensor_tensor(w[:], msk[:], -1e9, u[:], ALU.mult, ALU.add), 1)
        v.wait_ge(SV.h, SV.val)
        SV.inc(v.reduce_max(G8[:, 16:32], w.rearrange("p (t e) -> p t e", e=8), axis=AX.X), 1)
        v.wait_ge(SV.h, SV.val)
        for i in range(NT):
            inst = v.tensor_scalar(msk[:, i * 8:(i + 1) * 8], u[:, i * 8:(i + 1) * 8],
                                   G8[:, 16 + i:17 + i], None, ALU.subtract)
        SV.inc(inst, 1)
        v.wait_ge(SV.h, SV.val)
        SV.inc(v.tensor_scalar(msk[:], msk[:], 0.0, None, ALU.is_ge), 1)
        mk._gates_u_rdy = u_rdy

    u_rdy = mk._gates_u_rdy
    C0 = SC.val

    @block.scalar
    def _(sc):
        sc.wait_ge(SV.h, u_rdy)
        SC.inc(sc.activation(expu[:], u[:], AF.Exp), 1)

    @block.vector
    def _(v):
        v.wait_ge(SC.h, C0 + 1)
        v.wait_ge(SV.h, SV.val)
        SV.inc(v.tensor_mul(w[:], expu[:], msk[:]), 1)
        v.wait_ge(SV.h, SV.val)
        SV.inc(v.reduce_sum(G8[:, 32:48], w.rearrange("p (t e) -> p t e", e=8), axis=AX.X), 1)
        v.wait_ge(SV.h, SV.val)
        SV.inc(v.reciprocal(G8[:, 0:16], G8[:, 32:48]), 1)
        v.wait_ge(SV.h, SV.val)
        for i in range(NT):
            if scale2 is None:
                inst = v.tensor_scalar(cw_ap[:, i * 8:(i + 1) * 8], w[:, i * 8:(i + 1) * 8],
                                       G8[:, i:i + 1], None, ALU.mult)
            else:
                inst = v.tensor_scalar(cw_ap[:, i * 8:(i + 1) * 8], w[:, i * 8:(i + 1) * 8],
                                       G8[:, i:i + 1], scale2, ALU.mult, ALU.mult)
        SV.inc(inst, 1)
    return e_copy, SV.val



def emit_stage_B(mk):
    nc, block, d, s, pb = mk.nc, mk.block, mk.d, mk.s, mk.pb
    SG, SS, SM, SC, SV = (mk.sem[k] for k in ["g", "sy", "mm", "sc", "v"])
    P = mk.P
    S, S2 = s["stats"], s["stats2"]
    gate = mk.stage_gate

    acc = s["xres"]
    lnBT = mk.sb("lnBT_B", [128, CBN * NTOK], BF16, P + 65536)
    W2Tb = mk.sb("W2T_B", [128, 32 * 1024], BF16, P + 98304)
    w1dbl = mk.sb("w1dbl_B", [128, 2 * 4096], BF16, P + 163840)
    hstage = mk.sb("hstage_B", [128, 2 * 2048], BF16, P + 180224)
    hstream = mk.sb("hstream_B", [128, 2 * 8192], BF16, P + 163840)
    junkB = mk.sb("junk_B", [128, 1024], F32, P + 188416)
    stgB = mk.sb("stg_B", [128, 4 * 1024], F32, P + 163840)
    xrelB = [mk.sb(f"xrel{k}_B", [128, 1024], F32, P + 192512 + k * 4096) for k in range(2)]
    lnBrel = mk.sb("lnBrel_B", [128, NT * D], BF16, P + 98304)
    lnw_bc = mk.sb("lnw_bc", [128, 1024], F32, P + 131072)
    lnb_bc = mk.sb("lnb_bc", [128, 1024], F32, P + 135168)
    rb32 = mk.sb("rb32_B", [1, 2048], F32, P + 139264)

    def xt(i):
        return acc[:, i * D:(i + 1) * D]

    ev = emit_ln_stats(mk, xt, junkB[:, 0:1024], LN_EPS, wait_sems=[gate])

    @block.sync
    def _(sy):
        sy.wait_ge(SV.h, ev)
        SS.inc(sy.dma_start(out=d["xspill"].rearrange("(t p) c -> p t c", p=128),
                            in_=acc.rearrange("p (t c) -> p t c", c=D)), 16)
    xB_spilled = SS.val

    lnBT_in, _sgB = emit_ln_apply_bounce(mk, xt, stgB,
                                        lambda cb: lnBT[:, cb * NTOK:(cb + 1) * NTOK], ev)

    @block.vector
    def _(v):
        v.wait_ge(SS.h, lnBT_in)
        SV.inc(v.engine_nop(), 1)
    lnBT_rdy = SV.val

    @block.gpsimd
    def _(g):
        g.wait_ge(SS.h, xB_spilled)
        SG.inc(g.dma_start(out=s["wr8"].rearrange("p (cb r) -> p cb r", r=8),
                           in_=d["WrT"].rearrange("(cb p) r -> p cb r", p=128)), 16)
        SG.inc(g.dma_start(out=s["rowbuf"][:, 2048:2056], in_=mk.rowd("brB")), 16)
    wr_in = SG.val

    Mr = SM.val
    e_gcopy_pred = [SV.val + i + 1 for i in range(NT)]

    @block.tensor
    def _(t):
        t.wait_ge(SG.h, wr_in)
        t.wait_ge(SV.h, lnBT_rdy)
        for i in range(NT):
            if i >= 2:
                t.wait_ge(SV.h, e_gcopy_pred[i - 2])
            ps = pb[i % 2][:, 0:8]
            for cb in range(CBN):
                t.matmul(ps, lnBT[:, cb * NTOK + i * 128: cb * NTOK + (i + 1) * 128],
                         s["wr8"][:, cb * 8:(cb + 1) * 8], start=(cb == 0), stop=False)
            inst = t.matmul(ps, s["ones_c1"], s["rowbuf"][:, 2048:2056],
                            start=False, stop=True)
            SM.inc(inst, 1)

    e_gcopy, cw_done = emit_gates(mk, Mr, s["cw"], junkB[:, 0:512])
    assert e_gcopy == e_gcopy_pred

    ph2_prev = None
    ph2_prev_w6 = None
    ev_e = None
    for e in range(8):
        roff = (e % 2) * 1024

        @block.gpsimd
        def _(g, e=e, roff=roff, ph2_prev=ph2_prev):
            if e == 0:
                g.wait_ge(gate[0].h, gate[1])
            else:
                g.wait_ge(SM.h, ph2_prev)
            SG.inc(g.dma_start(out=s["rowbuf"][:, roff:roff + 1024],
                               in_=mk.rowd(f"b2B{e}")), 16)
            SG.inc(g.dma_start(out=W2Tb.rearrange("p (fb c) -> p fb c", c=1024),
                               in_=d["W2T"][e].rearrange("(fb p) c -> p fb c", p=128)), 16)
        w2_in = SG.val

        def evac_B(v, w, ntl, ps, chk, e=e):
            i = 2 * w + ntl
            dst = acc[:, i * D + chk * 512: i * D + (chk + 1) * 512]
            gcol = s["cw"][:, i * 8 + e: i * 8 + e + 1]
            if e == 0:
                return v.tensor_scalar(dst, ps, gcol, None, ALU.mult)
            return v.scalar_tensor_tensor(dst, ps, gcol, dst, ALU.mult, ALU.add)

        w1g = [(SV, cw_done)] if e == 0 else [(SM, ph2_prev_w6)]
        gg = [] if e == 0 else [(SM, ph2_prev)]
        ev_e, ph1_last, (ph2_last, ph2_w6) = emit_expert(
            mk, d["W1T"][e].rearrange("(cb p) f -> p cb f", p=128),
            d["W2T"][e].rearrange("(fb p) c -> p fb c", p=128),
            f"b1B{e}", s["rowbuf"][:, roff:roff + 1024],
            lnBT, w1dbl, hstage, W2Tb, hstream, evac_B,
            w1_gate=w1g, w2_gate=[(SG, w2_in)],
            rhs_gate=[(SV, cw_done)] if e == 0 else [],
            gelu_gate=gg,
        )
        ph2_prev = ph2_last
        ph2_prev_w6 = ph2_w6
    experts_done = ev_e

    @block.gpsimd
    def _(g):
        g.wait_ge(SM.h, ph2_prev)
        SG.inc(g.dma_start(out=lnBrel.rearrange("p (t c) -> p t c", c=D),
                           in_=d["tb"].rearrange("(t p) c -> p t c", p=128)), 16)
        SG.inc(g.dma_start(out=rb32[:, 0:1024], in_=mk.rowd("lnwB")), 16)
        SG.inc(g.dma_start(out=rb32[:, 1024:2048], in_=mk.rowd("lnbB")), 16)
    lnBrel_in = SG.val

    @block.tensor
    def _(t):
        t.wait_ge(SG.h, lnBrel_in)
        t.wait_ge(SV.h, experts_done)
        t.matmul(pb[0], s["ones_c1f"], rb32[:, 0:512], start=True, stop=True)
        t.matmul(pb[1], s["ones_c1f"], rb32[:, 512:1024], start=True, stop=True)
        t.matmul(pb[2], s["ones_c1f"], rb32[:, 1024:1536], start=True, stop=True)
        inst = t.matmul(pb[3], s["ones_c1f"], rb32[:, 1536:2048], start=True, stop=True)
        SM.inc(inst, 1)
    bc_mm = SM.val

    @block.scalar
    def _(sc):
        sc.wait_ge(SM.h, bc_mm)
        sc.copy(lnw_bc[:, 0:512], pb[0])
        sc.copy(lnw_bc[:, 512:1024], pb[1])
        sc.copy(lnb_bc[:, 0:512], pb[2])
        inst = sc.copy(lnb_bc[:, 512:1024], pb[3])
        SC.inc(inst, 1)
    bc_done = SC.val

    @block.vector
    def _(v):
        v.wait_ge(SV.h, experts_done)
        v.wait_ge(SG.h, lnBrel_in)
        inst = None
        for i in range(NT):
            inst = v.tensor_add(xt(i), xt(i), lnBrel[:, i * D:(i + 1) * D])
        SV.inc(inst, 1)
    z_done = SV.val

    ev2 = emit_ln_stats(mk, xt, junkB[:, 0:1024], B_EPS, wait_sems=[(SV, z_done)])

    lgB = SG.val
    e_xrelB = [lgB + 16 * (i + 1) for i in range(NT)]
    CF = SC.val
    VF = SV.val
    e_fin = [VF + 3 * i + 3 for i in range(NT)]

    @block.gpsimd
    def _(g):
        g.wait_ge(SV.h, z_done)
        for i in range(NT):
            if i >= 2:
                g.wait_ge(SV.h, e_fin[i - 2])
            SG.inc(g.dma_start(out=xrelB[i % 2], in_=d["xspill"][i * 128:(i + 1) * 128, :]), 16)

    @block.scalar
    def _(sc):
        sc.wait_ge(SV.h, ev2)
        sc.wait_ge(SC.h, bc_done)
        for i in range(NT):
            if i >= 1:
                sc.wait_ge(SV.h, e_fin[i - 1])
            inst = sc.activation(junkB[:, 0:1024], xt(i), AF.Identity,
                                 bias=S2[:, 16 + i:17 + i], scale=S2[:, i:i + 1])
            SC.inc(inst, 1)

    @block.vector
    def _(v):
        for i in range(NT):
            v.wait_ge(SC.h, CF + i + 1)
            v.wait_ge(SG.h, e_xrelB[i])
            SV.inc(v.tensor_mul(junkB[:, 0:1024], junkB[:, 0:1024], lnw_bc[:, 0:1024]), 1)
            v.wait_ge(SV.h, SV.val)
            SV.inc(v.tensor_add(junkB[:, 0:1024], junkB[:, 0:1024], lnb_bc[:, 0:1024]), 1)
            v.wait_ge(SV.h, SV.val)
            inst = v.tensor_add(xt(i), junkB[:, 0:1024], xrelB[i % 2][:, 0:1024])
            SV.inc(inst, 1)
    assert SV.val == e_fin[-1]
    mk.stage_gate = (SV, SV.val)



def emit_stage_C(mk):
    nc, block, d, s, pb = mk.nc, mk.block, mk.d, mk.s, mk.pb
    SG, SS, SM, SC, SV = (mk.sem[k] for k in ["g", "sy", "mm", "sc", "v"])
    P = mk.P
    S, S2 = s["stats"], s["stats2"]
    gate = mk.stage_gate
    NK = 1024

    xsb = s["xres"]
    A1 = mk.sb("A1_C", [128, CBN * NTOK], BF16, P + 65536)
    xdT = mk.sb("xdT_C", [128, CBN * NK], BF16, P + 98304)
    pbuf = mk.sb("p_C", [128, 8 * 512], BF16, P + 98304)
    ytmp = mk.sb("ytmp_C", [128, 512], BF16, P + 106496)
    screc = mk.sb("screc_C", [128, 1024], F32, P + 107520)
    qTC = mk.sb("qT_C", [128, CBN * NTOK], BF16, P + 114688)
    kTC = mk.sb("kT_C", [128, CBN * NK], BF16, P + 147456)
    vC = mk.sb("v_C", [128, 8 * 1040], BF16, P + 163840)
    stgC = mk.sb("stg_C", [128, 4 * 1024], F32, P + 163840)
    wslC = mk.sb("wsl_C", [128, CBN * 1024], BF16, P + 180608)
    junkC = mk.sb("junk_C", [128, 1024], F32, P + 196992)

    def xt(i):
        return xsb[:, i * D:(i + 1) * D]

    ev = emit_ln_stats(mk, xt, junkC[:, 0:1024], LN_EPS, wait_sems=[gate])
    lnCT_in, _sgC = emit_ln_apply_bounce(mk, xt, stgC,
                                        lambda cb: A1[:, cb * NTOK:(cb + 1) * NTOK], ev)

    @block.vector
    def _(v):
        v.wait_ge(SS.h, lnCT_in)
        for cb in range(CBN):
            src = A1[:, cb * NTOK:(cb + 1) * NTOK].rearrange("p (j s) -> p j s", s=2)
            dst = xdT[:, cb * NK:(cb + 1) * NK]
            inst = v.tensor_scalar(dst, src[:, :, 0:1].squeeze(2),
                                   mk.col("cw1", cb), mk.col("cbb", cb), ALU.mult, ALU.add)
        SV.inc(inst, 1)
        v.wait_ge(SV.h, SV.val)
        for cb in range(CBN):
            src = A1[:, cb * NTOK:(cb + 1) * NTOK].rearrange("p (j s) -> p j s", s=2)
            dst = xdT[:, cb * NK:(cb + 1) * NK]
            inst = v.scalar_tensor_tensor(dst, src[:, :, 1:2].squeeze(2), mk.col("cw2", cb),
                                          dst, ALU.mult, ALU.add)
        SV.inc(inst, 1)
        v.wait_ge(SV.h, SV.val)
        for cb in range(CBN):
            src = A1[:, cb * NTOK:(cb + 1) * NTOK].rearrange("p (j s) -> p j s", s=2)
            dst = xdT[:, cb * NK:(cb + 1) * NK]
            inst = v.scalar_tensor_tensor(dst[:, 1:1024], src[:, 0:1023, 1:2].squeeze(2),
                                          mk.col("cw0", cb), dst[:, 1:1024], ALU.mult, ALU.add)
        SV.inc(inst, 1)
    xdT_rdy = SV.val

    @block.gpsimd
    def _(g):
        g.wait_ge(gate[0].h, gate[1])
        SG.inc(g.dma_start(out=wslC.rearrange("p (cb k) -> p cb k", k=1024),
                           in_=d["qwT"].rearrange("(cb p) k -> p cb k", p=128)), 16)
        SG.inc(g.dma_start(out=s["rowbuf"][:, 0:1024], in_=mk.rowd("vbC")), 16)
        SG.inc(g.dma_start(out=s["rowbuf"][:, 1024:2048], in_=mk.rowd("outbC")), 16)
    qw_in = SG.val

    def gemm_feat(dst, rhs_src, bias_name, nch, src_len, waits):
        M0, C0 = SM.val, SC.val

        @block.tensor
        def _(t):
            for se, val in waits:
                t.wait_ge(se.h, val)
            for j in range(CBN * nch):
                ob, ch = divmod(j, nch)
                if j >= 6:
                    t.wait_ge(SC.h, C0 + (j - 6) + 1)
                ps = pb[j % 6]
                for cb in range(CBN):
                    inst = t.matmul(ps, wslC[:, cb * 1024 + ob * 128: cb * 1024 + (ob + 1) * 128],
                                    rhs_src[:, cb * src_len + ch * 512: cb * src_len + (ch + 1) * 512],
                                    start=(cb == 0), stop=(cb == CBN - 1))
                SM.inc(inst, 1)

        @block.scalar
        def _(sc):
            for j in range(CBN * nch):
                ob, ch = divmod(j, nch)
                sc.wait_ge(SM.h, M0 + j + 1)
                inst = sc.activation(dst[:, ob * src_len + ch * 512:
                                         ob * src_len + (ch + 1) * 512],
                                     pb[j % 6], AF.Identity, bias=mk.col(bias_name, ob))
                SC.inc(inst, 1)
        return SM.val, SC.val

    Mq_end, q_done = gemm_feat(qTC, A1, "qbC", 4, NTOK, [(SG, qw_in), (SS, lnCT_in)])

    @block.gpsimd
    def _(g):
        g.wait_ge(SM.h, Mq_end)
        SG.inc(g.dma_start(out=wslC.rearrange("p (cb k) -> p cb k", k=1024),
                           in_=d["kwT"].rearrange("(cb p) k -> p cb k", p=128)), 16)
    kw_in = SG.val

    Mk_end, k_done = gemm_feat(kTC, xdT, "kbC", 2, NK, [(SG, kw_in), (SV, xdT_rdy), (SC, q_done)])

    @block.gpsimd
    def _(g):
        g.wait_ge(SM.h, Mk_end)
        SG.inc(g.dma_start(out=wslC.rearrange("p (cb k) -> p cb k", k=1024),
                           in_=d["vwT"].rearrange("(cb p) k -> p cb k", p=128)), 16)
    vw_in = SG.val

    @block.vector
    def _(v):
        v.wait_ge(SC.h, k_done)
        for mt in range(8):
            for h in range(NH):
                inst = v.memset(vC[:, mt * 1040 + h * 65 + 64: mt * 1040 + h * 65 + 65], 1.0)
        SV.inc(inst, 1)
    vones = SV.val

    Mv, Vv = SM.val, SV.val

    @block.tensor
    def _(t):
        t.wait_ge(SG.h, vw_in)
        t.wait_ge(SV.h, vones)
        t.wait_ge(SC.h, k_done)
        for j in range(16):
            mt, chk = divmod(j, 2)
            if j >= 6:
                t.wait_ge(SV.h, Vv + (j - 6) + 1)
            ps = pb[j % 6]
            for cb in range(CBN):
                t.matmul(ps, xdT[:, cb * NK + mt * 128: cb * NK + (mt + 1) * 128],
                         wslC[:, cb * 1024 + chk * 512: cb * 1024 + (chk + 1) * 512],
                         start=(cb == 0), stop=False)
            inst = t.matmul(ps, s["ones_c1"], s["rowbuf"][:, chk * 512:(chk + 1) * 512],
                            start=False, stop=True)
            SM.inc(inst, 1)

    @block.vector
    def _(v):
        for j in range(16):
            mt, chk = divmod(j, 2)
            v.wait_ge(SM.h, Mv + j + 1)
            inst = v.tensor_copy(
                vC[:, mt * 1040:(mt + 1) * 1040]
                .rearrange("p (h c) -> p h c", c=65)[:, chk * 8:(chk + 1) * 8, 0:64],
                pb[j % 6].rearrange("p (h c) -> p h c", c=64))
            SV.inc(inst, 1)
    v_done = SV.val

    @block.gpsimd
    def _(g):
        g.wait_ge(SM.h, Mv + 16)
        SG.inc(g.dma_start(out=wslC.rearrange("p (cb k) -> p cb k", k=1024),
                           in_=d["owT"].rearrange("(cb p) k -> p cb k", p=128)), 16)
    ow_in = SG.val

    jobs = [(h, ch) for h in range(NH) for ch in range(4)]
    Ma, Va, Ca = SM.val, SV.val, SC.val
    e_qk, e_av, e_rb, e_shift = {}, {}, {}, {}
    cnt = Ma
    for (h, ch) in jobs:
        for mb in range(8):
            cnt += 1
            e_qk[(h, ch, mb)] = cnt
        cnt += 1
        e_av[(h, ch)] = cnt
        cnt += 1
        e_rb[(h, ch)] = cnt
        if h % 2 == 1:
            cnt += 1
            e_shift[(h, ch)] = cnt
    e_exp, e_recS = {}, {}
    cnt = Ca
    for (h, ch) in jobs:
        for mb in range(8):
            cnt += 1
            e_exp[(h, ch, mb)] = cnt
        cnt += 2
        e_recS[(h, ch)] = cnt
    e_rbc, e_ymul, e_fin = {}, {}, {}
    cnt = Va
    for (h, ch) in jobs:
        cnt += 1
        e_rbc[(h, ch)] = cnt
        cnt += 1
        e_ymul[(h, ch)] = cnt
        if h % 2 == 1:
            cnt += 1
            e_fin[(h, ch)] = cnt
        else:
            e_fin[(h, ch)] = cnt
    v_att_end = cnt

    def prev_job(idx):
        return jobs[idx - 1] if idx >= 1 else None

    @block.tensor
    def _(t):
        t.wait_ge(SC.h, q_done)
        t.wait_ge(SV.h, v_done)
        for jidx, (h, ch) in enumerate(jobs):
            grp = h % 2
            for mb in range(8):
                gq = jidx * 8 + mb
                if gq >= 2:
                    ph, pc, pm = jobs[(gq - 2) // 8][0], jobs[(gq - 2) // 8][1], (gq - 2) % 8
                    t.wait_ge(SC.h, e_exp[(ph, pc, pm)])
                lh = kTC[64 * grp:64 * grp + 64,
                         (h // 2) * NK + mb * 128:(h // 2) * NK + (mb + 1) * 128]
                rh = qTC[64 * grp:64 * grp + 64,
                         (h // 2) * NTOK + ch * 512:(h // 2) * NTOK + (ch + 1) * 512]
                inst = t.matmul(pb[mb % 2], lh, rh, start=True, stop=True,
                                tile_position=(64 * grp, 0))
                SM.inc(inst, 1)
            pj = prev_job(jidx)
            if pj is not None:
                t.wait_ge(SV.h, e_ymul[pj])
            for mb in range(8):
                t.wait_ge(SC.h, e_exp[(h, ch, mb)])
                inst = t.matmul(pb[2][0:65, 0:512],
                                vC[:, mb * 1040 + h * 65: mb * 1040 + h * 65 + 65],
                                pbuf[:, mb * 512:(mb + 1) * 512],
                                start=(mb == 0), stop=(mb == 7))
            SM.inc(inst, 1)
            t.wait_ge(SC.h, e_recS[(h, ch)])
            inst = t.matmul(pb[4][0:64, 0:512], s["onep64"][64:65, :],
                            screc[64:65, 0:512], start=True, stop=True,
                            tile_position=(64, 0))
            SM.inc(inst, 1)
            if h % 2 == 1:
                t.wait_ge(SV.h, e_ymul[(h, ch)])
                inst = t.matmul(pb[3], s["ishift"][0:64, :], ytmp[0:64, 0:512],
                                start=True, stop=True)
                SM.inc(inst, 1)

    @block.scalar
    def _(sc):
        for jidx, (h, ch) in enumerate(jobs):
            for mb in range(8):
                sc.wait_ge(SM.h, e_qk[(h, ch, mb)])
                if jidx >= 1:
                    sc.wait_ge(SM.h, e_av[prev_job(jidx)])
                inst = sc.activation(pbuf[:, mb * 512:(mb + 1) * 512],
                                     pb[mb % 2], AF.Exp, scale=0.125)
                SC.inc(inst, 1)
            sc.wait_ge(SM.h, e_av[(h, ch)])
            if jidx >= 1:
                sc.wait_ge(SM.h, e_rb[prev_job(jidx)])
            SC.inc(sc.activation(screc[64:65, 512:1024], pb[2][64:65, 0:512], AF.Ln), 1)
            sc.wait_ge(SC.h, SC.val)
            inst = sc.activation(screc[64:65, 0:512], screc[64:65, 512:1024],
                                 AF.Exp, scale=-1.0)
            SC.inc(inst, 1)

    @block.vector
    def _(v):
        for jidx, (h, ch) in enumerate(jobs):
            grp = h % 2
            v.wait_ge(SM.h, e_rb[(h, ch)])
            inst = v.tensor_copy(screc[0:64, 512:1024], pb[4][0:64, 0:512])
            SV.inc(inst, 1)
            v.wait_ge(SV.h, e_rbc[(h, ch)])
            if grp == 0:
                dstY = A1[0:64, (h // 2) * NTOK + ch * 512:(h // 2) * NTOK + (ch + 1) * 512]
                inst = v.tensor_tensor(dstY, pb[2][0:64, 0:512], screc[0:64, 512:1024], ALU.mult)
                SV.inc(inst, 1)
            else:
                inst = v.tensor_tensor(ytmp[0:64, 0:512], pb[2][0:64, 0:512],
                                       screc[0:64, 512:1024], ALU.mult)
                SV.inc(inst, 1)
                v.wait_ge(SM.h, e_shift[(h, ch)])
                dstY = A1[64:128, (h // 2) * NTOK + ch * 512:(h // 2) * NTOK + (ch + 1) * 512]
                inst = v.tensor_copy(dstY, pb[3][64:128, 0:512])
                SV.inc(inst, 1)
    assert SV.val == v_att_end

    M4, V4 = SM.val, SV.val

    @block.tensor
    def _(t):
        t.wait_ge(SV.h, v_att_end)
        t.wait_ge(SG.h, ow_in)
        for i in range(NT):
            if i >= 3:
                t.wait_ge(SV.h, V4 + (i - 3) + 1)
            for chk in range(2):
                ps = pb[(i % 3) * 2 + chk]
                for cb in range(CBN):
                    t.matmul(ps, A1[:, cb * NTOK + i * 128: cb * NTOK + (i + 1) * 128],
                             wslC[:, cb * 1024 + chk * 512: cb * 1024 + (chk + 1) * 512],
                             start=(cb == 0), stop=False)
                inst = t.matmul(ps, s["ones_c1"],
                                s["rowbuf"][:, 1024 + chk * 512: 1024 + (chk + 1) * 512],
                                start=False, stop=True)
            SM.inc(inst, 1)

    @block.vector
    def _(v):
        for i in range(NT):
            v.wait_ge(SM.h, M4 + i + 1)
            for chk in range(2):
                dst = xt(i)[:, chk * 512:(chk + 1) * 512]
                inst = v.scalar_tensor_tensor(dst, pb[(i % 3) * 2 + chk], 1.0, dst,
                                              ALU.mult, ALU.add)
            SV.inc(inst, 1)
    mk.stage_gate = (SV, SV.val)



def emit_stage_D(mk):
    nc, block, d, s, pb = mk.nc, mk.block, mk.d, mk.s, mk.pb
    SG, SS, SM, SC, SV = (mk.sem[k] for k in ["g", "sy", "mm", "sc", "v"])
    P = mk.P
    S, S2 = s["stats"], s["stats2"]
    gate = mk.stage_gate

    xsb = s["xres"]
    lnDT = mk.sb("lnDT_D", [128, CBN * NTOK], BF16, P + 65536)
    wslD = mk.sb("wsl_D", [128, CBN * 1024], BF16, P + 98304)
    stgD = mk.sb("stg_D", [128, 4 * 1024], F32, P + 98304)
    AcatS = mk.sb("Acat_D", [128, CBN * 128], BF16, P + 114688)
    BcatS = mk.sb("Bcat_D", [128, 1024], BF16, P + 116736)
    axS = mk.sb("ax_D", [128, NT * 128], F32, P + 118784)
    axwS = mk.sb("axw_D", [128, NT * 128], BF16, P + 126976)
    axwT = mk.sb("axwT_D", [128, NTOK], BF16, P + 131072)
    junkD = mk.sb("junk_D", [128, 1024], F32, P + 135168)

    def xt(i):
        return xsb[:, i * D:(i + 1) * D]

    ev = emit_ln_stats(mk, xt, junkD[:, 0:1024], LN_EPS, wait_sems=[gate])
    lnDT_in, _sgD = emit_ln_apply_bounce(mk, xt, stgD,
                                        lambda cb: lnDT[:, cb * NTOK:(cb + 1) * NTOK], ev)

    @block.gpsimd
    def _(g):
        g.wait_ge(SS.h, lnDT_in)
        SG.inc(g.dma_start(out=wslD.rearrange("p (cb k) -> p cb k", k=1024),
                           in_=d["bwT"].rearrange("(cb p) k -> p cb k", p=128)), 16)
        SG.inc(g.dma_start(out=AcatS.rearrange("p (cb r) -> p cb r", r=128),
                           in_=d["Acat"].rearrange("(cb p) r -> p cb r", p=128)), 16)
        SG.inc(g.dma_start(out=BcatS, in_=d["Bcat"]), 16)
        SG.inc(g.dma_start(out=s["wr8"].rearrange("p (cb r) -> p cb r", r=8),
                           in_=d["rwT"].rearrange("(cb p) r -> p cb r", p=128)), 16)
        SG.inc(g.dma_start(out=s["rowbuf"][:, 0:1024], in_=mk.rowd("bbD")), 16)
        SG.inc(g.dma_start(out=s["rowbuf"][:, 1024:1032], in_=mk.rowd("brD")), 16)
    ld_in = SG.val

    Mr = SM.val
    e_gcopy_pred = [SV.val + i + 1 for i in range(NT)]

    @block.tensor
    def _(t):
        t.wait_ge(SG.h, ld_in)
        t.wait_ge(SS.h, lnDT_in)
        for i in range(NT):
            if i >= 2:
                t.wait_ge(SV.h, e_gcopy_pred[i - 2])
            ps = pb[i % 2][:, 0:8]
            for cb in range(CBN):
                t.matmul(ps, lnDT[:, cb * NTOK + i * 128: cb * NTOK + (i + 1) * 128],
                         s["wr8"][:, cb * 8:(cb + 1) * 8], start=(cb == 0), stop=False)
            inst = t.matmul(ps, s["ones_c1"], s["rowbuf"][:, 1024:1032],
                            start=False, stop=True)
            SM.inc(inst, 1)

    e_gcopy, cwD_done = emit_gates(mk, Mr, s["cwD"], junkD[:, 0:512], scale2=2.0)
    assert e_gcopy == e_gcopy_pred

    Ma = SM.val
    e_ax = [Ma + i + 1 for i in range(NT)]
    Vax = SV.val
    e_axev = [Vax + 2 * i + 2 for i in range(NT)]

    @block.tensor
    def _(t):
        for i in range(NT):
            if i >= 2:
                t.wait_ge(SV.h, e_axev[i - 2])
            ps2 = pb[2 + i % 2]
            for cb in range(CBN):
                inst = t.matmul(ps2[:, 0:128],
                                lnDT[:, cb * NTOK + i * 128: cb * NTOK + (i + 1) * 128],
                                AcatS[:, cb * 128:(cb + 1) * 128],
                                start=(cb == 0), stop=(cb == CBN - 1))
            SM.inc(inst, 1)

    @block.vector
    def _(v):
        v.wait_ge(SV.h, cwD_done)
        for i in range(NT):
            v.wait_ge(SM.h, e_ax[i])
            SV.inc(v.tensor_copy(axS[:, i * 128:(i + 1) * 128], pb[2 + i % 2][:, 0:128]), 1)
            v.wait_ge(SV.h, SV.val)
            for e in range(8):
                inst = v.tensor_scalar(axwS[:, i * 128 + e * 16: i * 128 + (e + 1) * 16],
                                       axS[:, i * 128 + e * 16: i * 128 + (e + 1) * 16],
                                       s["cwD"][:, i * 8 + e: i * 8 + e + 1], None, ALU.mult)
            SV.inc(inst, 1)
    axw_done = SV.val

    axwT_in = bounce_transpose(mk, axwS.rearrange("p (t r) -> p t r", r=128),
                               lambda cb: axwT[:, 0:NTOK],
                               wait=(SV, axw_done), ncols=128)

    M5, V5 = SM.val, SV.val

    @block.tensor
    def _(t):
        t.wait_ge(SS.h, axwT_in)
        for i in range(NT):
            if i >= 3:
                t.wait_ge(SV.h, V5 + (i - 3) + 1)
            for chk in range(2):
                ps = pb[(i % 3) * 2 + chk]
                for cb in range(CBN):
                    t.matmul(ps, lnDT[:, cb * NTOK + i * 128: cb * NTOK + (i + 1) * 128],
                             wslD[:, cb * 1024 + chk * 512: cb * 1024 + (chk + 1) * 512],
                             start=(cb == 0), stop=False)
                t.matmul(ps, s["ones_c1"], s["rowbuf"][:, chk * 512:(chk + 1) * 512],
                         start=False, stop=False)
                inst = t.matmul(ps, axwT[:, i * 128:(i + 1) * 128],
                                BcatS[:, chk * 512:(chk + 1) * 512],
                                start=False, stop=True)
            SM.inc(inst, 1)

    @block.vector
    def _(v):
        for i in range(NT):
            v.wait_ge(SM.h, M5 + i + 1)
            for chk in range(2):
                dst = xt(i)[:, chk * 512:(chk + 1) * 512]
                inst = v.scalar_tensor_tensor(dst, pb[(i % 3) * 2 + chk], 1.0, dst,
                                              ALU.mult, ALU.add)
            SV.inc(inst, 1)
    mk.stage_gate = (SV, SV.val)


def emit_finish(mk):
    block, d = mk.block, mk.d
    SS = mk.sem["sy"]
    gate = mk.stage_gate
    fin = mk.s["xres"]

    @block.sync
    def _(sy):
        sy.wait_ge(gate[0].h, gate[1])
        SS.inc(sy.dma_start(out=d["out"].rearrange("(t p) c -> p t c", p=128),
                            in_=fin.rearrange("p (t c) -> p t c", c=D)), 16)
        sy.wait_ge(SS.h, SS.val)



def run(inputs, stages=("A", "B", "C", "D"), cores=8):
    c, xs, colmap, rowmap = host_prepare(inputs)
    nc, mk = build_program(colmap, rowmap, stages)
    in_maps = []
    for b in range(cores):
        m = dict(c)
        m["x"] = xs[b]
        in_maps.append(m)
    res = run_bass_kernel_spmd(nc, in_maps, core_ids=list(range(cores)))
    out = np.stack([res.results[b]["out"] for b in range(cores)], axis=0)
    return out


def kernel(**inputs):
    c, xs, colmap, rowmap = host_prepare(inputs)
    nc, mk = build_program(colmap, rowmap, ("A", "B", "C", "D"))
    in_maps = []
    for b in range(8):
        m = dict(c)
        m["x"] = xs[b]
        in_maps.append(m)
    res = run_bass_kernel_spmd(nc, in_maps, core_ids=list(range(8)))
    return np.stack([res.results[b]["out"] for b in range(8)], axis=0)
